# version 11
# baseline (speedup 1.0000x reference)
import numpy as np
import ml_dtypes
import concourse.bass as bass
import concourse.mybir as mybir
from concourse.bass_utils import run_bass_kernel_spmd

F32 = mybir.dt.float32
BF16 = mybir.dt.bfloat16
AF = mybir.ActivationFunctionType
ALU = mybir.AluOpType
BF = ml_dtypes.bfloat16

NCORE = 8
TOK = 1024
NK = 16
EXP_SCALE = 0.08838834764831845
LN_EPS = 1e-5

ENGINES = ("sync", "gpsimd", "tensor", "scalar", "vector")


class Res:
    __slots__ = ("w", "rd")

    def __init__(self):
        self.w = None
        self.rd = {}


class Sched:
    def __init__(self, nc, paranoid=False):
        self.nc = nc
        self.paranoid = paranoid
        self.cnt = {e: 0 for e in ENGINES}
        self.know = {e: {p: 0 for p in ENGINES} for e in ENGINES}
        self.self_done = {e: 0 for e in ENGINES}
        self.lastwait = {}
        self.snaps = {e: [] for e in ENGINES}
        self.ops = {e: [] for e in ENGINES}
        self.sem = {e: nc.alloc_semaphore("P_" + e) for e in ENGINES}

    def _need(self, eng, deps, is_dma):
        need = {}
        selfneed = 0
        for ref, force in deps:
            if ref is None:
                continue
            peng, pcnt, pdma = ref
            if peng == eng:
                if force or (pdma and not is_dma):
                    selfneed = max(selfneed, pcnt)
                continue
            if force:
                pass
            if need.get(peng, 0) < pcnt:
                need[peng] = pcnt
        waits = []
        k = self.know[eng]
        for peng, pcnt in sorted(need.items()):
            if k[peng] >= pcnt and not self.paranoid:
                continue
            if self.paranoid and self.lastwait.setdefault(eng, {}).get(peng, 0) >= pcnt:
                continue
            waits.append((self.sem[peng], pcnt))
            for (v, kn) in self.snaps[peng]:
                if v >= pcnt:
                    for e2, v2 in kn.items():
                        if k[e2] < v2:
                            k[e2] = v2
                    break
            if k[peng] < pcnt:
                k[peng] = pcnt
        if self.paranoid:
            for (s, v) in waits:
                for e2 in ENGINES:
                    if s is self.sem[e2]:
                        self.lastwait.setdefault(eng, {})[e2] = max(
                            self.lastwait.setdefault(eng, {}).get(e2, 0), v)
        if selfneed > self.self_done[eng]:
            waits.append((self.sem[eng], selfneed))
            self.self_done[eng] = selfneed
        return waits

    def op(self, eng, emit, reads=(), writes=(), sreads=(), dma=False, inc=None):
        deps = []
        for r in reads:
            deps.append((r.w, False))
        for r in sreads:
            deps.append((r.w, True))
        for r in writes:
            deps.append((r.w, False))
            for ref in r.rd.values():
                deps.append((ref, False))
        waits = self._need(eng, deps, dma)
        if inc is None:
            inc = 16 if dma else 1
        self.cnt[eng] += inc
        myref = (eng, self.cnt[eng], dma)
        self.know[eng][eng] = self.cnt[eng]
        self.snaps[eng].append((self.cnt[eng], dict(self.know[eng])))
        for r in reads:
            r.rd[eng] = myref
        for r in sreads:
            r.rd[eng] = myref
        for r in writes:
            r.w = myref
            r.rd = {}
        self.ops[eng].append((waits, emit, inc, self.sem[eng]))

    def replay(self, eng_name):
        sem_tbl = self.sem

        def body(e):
            for waits, emit, inc, sem in self.ops[eng_name]:
                for (s, v) in waits:
                    e.wait_ge(s, v)
                inst = emit(e)
                inst.then_inc(sem, inc)
        return body


def build_nc(debug=False, paranoid=False, inject_scale=False, inject_sctok=False, inject_lgt=False):
    nc = bass.Bass("TRN2", target_bir_lowering=False, dynamic_dma_scratch_size=8192)

    d_xt = nc.dram_tensor("xt", [2048, TOK], F32, kind="ExternalInput").ap()
    d_xpt = nc.dram_tensor("xpt", [2048, TOK], F32, kind="ExternalInput").ap()
    d_wqkv = nc.dram_tensor("wqkv", [48, 128, 2048], F32, kind="ExternalInput").ap()
    d_bqkv = nc.dram_tensor("bqkv", [128, 48], F32, kind="ExternalInput").ap()
    d_wproj = nc.dram_tensor("wproj", [16, 128, 2048], F32, kind="ExternalInput").ap()
    d_bproj = nc.dram_tensor("bproj", [128, 16], F32, kind="ExternalInput").ap()
    d_gate = nc.dram_tensor("gate", [128, 128], F32, kind="ExternalInput").ap()
    d_wup = nc.dram_tensor("wup", [64, 128, 2048], F32, kind="ExternalInput").ap()
    d_wdown = nc.dram_tensor("wdown", [64, 128, 2048], F32, kind="ExternalInput").ap()
    d_wexp = nc.dram_tensor("wexp", [16, 128, 2048], F32, kind="ExternalInput").ap()
    d_gam = nc.dram_tensor("gam", [128, 16], F32, kind="ExternalInput").ap()
    d_bet = nc.dram_tensor("bet", [128, 16], F32, kind="ExternalInput").ap()
    d_mask = nc.dram_tensor("mask", [128, 128], BF16, kind="ExternalInput").ap()
    d_ident = nc.dram_tensor("ident", [128, 128], BF16, kind="ExternalInput").ap()
    d_yt = nc.dram_tensor("yt", [2048, TOK], F32, kind="ExternalOutput").ap()
    d_scin = (nc.dram_tensor("scin", [128, TOK], F32, kind="ExternalInput").ap()
              if inject_scale else None)
    d_sctin = (nc.dram_tensor("sctin", [128, 8], F32, kind="ExternalInput").ap()
               if inject_sctok else None)
    d_lgtin = (nc.dram_tensor("lgtin", [128, 64], F32, kind="ExternalInput").ap()
               if inject_lgt else None)
    d_x1 = nc.dram_tensor("x1scr", [2048, TOK], F32).ap()
    ddump = {}
    if debug:
        for nm in ("dln1", "dqg", "dkg", "dvt", "dvsb", "dctx", "dctxT",
                   "dattnT", "dbT", "dy", "df"):
            ddump[nm] = nc.dram_tensor(nm, [128, 16 * TOK], BF16, kind="ExternalOutput").ap()
        ddump["dx1"] = nc.dram_tensor("dx1", [2048, TOK], F32, kind="ExternalOutput").ap()
        ddump["dscbc"] = nc.dram_tensor("dscbc", [128, TOK], F32, kind="ExternalOutput").ap()
        ddump["dsct"] = nc.dram_tensor("dsct", [128, 8], F32, kind="ExternalOutput").ap()
        ddump["dlgt"] = nc.dram_tensor("dlgt", [128, 8], F32, kind="ExternalOutput").ap()
        ddump["dscstT"] = nc.dram_tensor("dscstT", [8, 128], F32, kind="ExternalOutput").ap()
        ddump["dlgtall"] = nc.dram_tensor("dlgtall", [128, 64], F32, kind="ExternalOutput").ap()
        ddump["dgbf"] = nc.dram_tensor("dgbf", [128, 128], BF16, kind="ExternalOutput").ap()
        ddump["dgstg"] = nc.dram_tensor("dgstg", [128, 128], F32, kind="ExternalOutput").ap()
        ddump["dmx8all"] = nc.dram_tensor("dmx8all", [128, 64], F32, kind="ExternalOutput").ap()

    A = nc.alloc_sbuf_tensor
    s1 = A("s1", [128, 16 * TOK], BF16).ap()
    s2 = A("s2", [128, 16 * TOK], BF16).ap()
    s3 = A("s3", [128, 16 * TOK], BF16).ap()
    s4 = A("s4", [128, 16 * TOK], BF16).ap()
    mu = A("mu", [128, TOK], F32).ap()
    rst = A("rst", [128, TOK], F32).ap()
    tA = [A(f"tA{i}", [128, TOK], F32).ap() for i in range(2)]
    ex2 = tA[0]
    var = tA[1]
    xstg = [A(f"xstg{i}", [128, TOK], F32).ap() for i in range(2)]
    wstg = [A(f"wstg{i}", [128, 2048], F32).ap() for i in range(2)]
    scbc = A("scbc", [128, TOK], F32).ap()
    wbf = [A(f"wbf{i}", [128, 2048], BF16).ap() for i in range(2)]
    cbf = [A(f"cbf{i}", [128, TOK], BF16).ap() for i in range(2)]
    sqb = [A(f"sqb{i}", [128, TOK], BF16).ap() for i in range(2)]
    rlb = [A(f"rlb{i}", [128, TOK], BF16).ap() for i in range(2)]
    ptg = [A(f"ptg{i}", [128, 128], BF16).ap() for i in range(2)]
    vbg = [A(f"vbg{i}", [128, 132], BF16).ap() for i in range(2)]
    onesbf = A("onesbf", [128, 128], BF16).ap()
    mask = A("maskt", [128, 128], BF16).ap()
    ident = A("identt", [128, 128], BF16).ap()
    identf = A("identf", [128, 128], F32).ap()
    gbf = A("gbf", [128, 128], BF16).ap()
    bqkv = A("bqkvt", [128, 48], F32).ap()
    bproj = A("bprojt", [128, 16], F32).ap()
    gam = A("gamt", [128, 16], F32).ap()
    bet = A("bett", [128, 16], F32).ap()
    gstg = A("gstg", [128, 128], F32).ap()
    denall = A("denall", [128, 128], F32).ap()
    recal = A("recal", [128, 128], F32).ap()
    lgtB = A("lgtB", [128, 64], F32).ap()
    s28B = A("s28B", [128, 256], F32).ap()
    s28dB = A("s28dB", [128, 128], F32).ap()
    s28sB = A("s28sB", [128, 128], F32).ap()
    s28aB = A("s28aB", [128, 128], F32).ap()
    sctok = A("sctok", [128, 8], F32).ap()
    scstT = A("scstT", [8, 128], F32).ap()
    lgT = A("lgT", [8, TOK], F32).ap()
    scT = scbc[0:1, :]
    ones1f = A("ones1f", [1, 128], F32).ap()
    epst = A("epst", [128, 1], F32).ap()
    dly = A("dly", [128, 1], F32).ap()

    if debug:
        lgtall = A("lgtall", [128, 64], F32).ap()
        mx8all = A("mx8all", [128, 64], F32).ap()
        r_lgtall = Res(); r_mx8all = Res()
    P = nc.alloc_psum_tensor
    psA = P("psA", [128, TOK], F32).ap()
    psB = P("psB", [128, TOK], F32).ap()
    psC = P("psC", [128, TOK], F32).ap()
    psT1 = P("psT1", [128, 512], BF16).ap()
    psT2 = P("psT2", [128, 512], BF16).ap()

    S = Sched(nc, paranoid=paranoid)

    def RR():
        return Res()

    r_s1 = RR(); r_s2 = RR(); r_s3 = RR(); r_s4 = RR()
    r_mu = RR(); r_rst = RR()
    r_tA = [RR(), RR()]
    r_ex2 = r_tA[0]; r_var = r_tA[1]
    r_xstg = [RR(), RR()]; r_wstg = [RR(), RR()]
    r_wbf = [RR(), RR()]; r_cbf = [RR(), RR()]; r_sqb = [RR(), RR()]
    r_rlb = r_cbf
    r_ptg = [RR(), RR()]; r_vbg = [RR(), RR()]
    r_scbc = RR(); r_small = RR()
    r_den = RR(); r_recal = RR()
    r_gate = RR()
    r_lgT = RR()
    r_lgtB = RR(); r_s28B = RR()
    r_s28dB = RR(); r_s28sB = RR(); r_s28aB = RR()
    r_psA0 = RR(); r_psA1 = RR()
    r_psB0 = RR(); r_psB1 = RR()
    r_psC0 = RR(); r_psC1 = RR()
    r_psA = [r_psA0, r_psA1]
    r_psB = [r_psB0, r_psB1]
    r_psCb0 = r_psC0; r_psCb1 = r_psC1
    r_psT = [RR(), RR()]
    r_x1 = RR()
    r_yt = RR()
    r_psAb = [r_psA0, r_psA1]

    ps2 = [r_psA, r_psB]

    S.op("sync", lambda e: e.dma_start(out=mask[:, :], in_=d_mask[:, :]), writes=[r_small], dma=True)
    S.op("sync", lambda e: e.dma_start(out=ident[:, :], in_=d_ident[:, :]), writes=[r_small], dma=True)
    S.op("sync", lambda e: e.dma_start(out=bqkv[:, :], in_=d_bqkv[:, :]), writes=[r_small], dma=True)
    S.op("sync", lambda e: e.dma_start(out=bproj[:, :], in_=d_bproj[:, :]), writes=[r_small], dma=True)
    S.op("sync", lambda e: e.dma_start(out=gam[:, :], in_=d_gam[:, :]), writes=[r_small], dma=True)
    S.op("sync", lambda e: e.dma_start(out=bet[:, :], in_=d_bet[:, :]), writes=[r_small], dma=True)
    S.op("sync", lambda e: e.dma_start(out=gstg[:, :], in_=d_gate[:, :]), writes=[r_small], dma=True)
    S.op("vector", lambda e: e.memset(onesbf[:, :], 1.0 / 2048.0), writes=[r_small])
    S.op("vector", lambda e: e.memset(vbg[0][:, 128:129], 1.0), writes=[r_vbg[0]])
    S.op("vector", lambda e: e.memset(vbg[1][:, 128:129], 1.0), writes=[r_vbg[1]])
    S.op("vector", lambda e: e.memset(ones1f[:, :], 1.0), writes=[r_small])
    S.op("vector", lambda e: e.memset(epst[:, :], LN_EPS), writes=[r_small])
    S.op("vector", lambda e: e.tensor_copy(gbf[:, :], gstg[:, :]), reads=[r_small], writes=[r_small])
    S.op("scalar", lambda e: e.copy(identf[:, :], ident[:, :]), reads=[r_small], writes=[r_small])

    def stats_from(tiles_ap, tiles_res, cast_needed, src_f32=None, src_res=None):
        for k in range(16):
            al = k % 2
            if cast_needed:
                dma_emit, res_in = src_f32(k)
                S.op("gpsimd", dma_emit, reads=[res_in] if res_in else [], writes=[r_xstg[al]], dma=True)
                S.op("vector", (lambda al=al: lambda e: e.tensor_copy(cbf[al][:, :], xstg[al][:, :]))(),
                     reads=[r_xstg[al]], writes=[r_cbf[al]])
                S.op("scalar", (lambda al=al: lambda e: e.square(sqb[al][:, :], xstg[al][:, :]))(),
                     reads=[r_xstg[al]], writes=[r_sqb[al]])
                xa, xr = cbf[al], r_cbf[al]
            else:
                xa = tiles_ap(k)
                xr = tiles_res
                S.op("scalar", (lambda xa=xa, al=al: lambda e: e.square(sqb[al][:, :], xa))(),
                     reads=[xr], writes=[r_sqb[al]])
            def mm_sum(e, dst, src, k=k):
                i0 = None
                for hhalf in range(2):
                    i0 = e.matmul(dst[:, hhalf * 512:(hhalf + 1) * 512],
                                  onesbf[:, :], src[:, hhalf * 512:(hhalf + 1) * 512],
                                  start=(k == 0), stop=(k == 15))
                return i0
            S.op("tensor", (lambda xa=xa, k=k: lambda e: mm_sum(e, psA, xa, k))(),
                 reads=[xr, r_small], writes=r_psA)
            S.op("tensor", (lambda al=al, k=k: lambda e: mm_sum(e, psB, sqb[al][:, :], k))(),
                 reads=[r_sqb[al], r_small], writes=r_psB)
        S.op("scalar", lambda e: e.copy(mu[:, :], psA[:, :]), reads=r_psA, writes=[r_mu])
        S.op("scalar", lambda e: e.copy(ex2[:, :], psB[:, :]), reads=r_psB, writes=[r_ex2])
        S.op("vector", lambda e: e.tensor_mul(var[:, :], mu[:, :], mu[:, :]),
             reads=[r_mu], writes=[r_var])
        S.op("vector", lambda e: e.tensor_sub(var[:, :], ex2[:, :], var[:, :]),
             reads=[r_ex2, r_var], writes=[r_var])
        S.op("scalar", lambda e: e.activation(var[:, :], var[:, :], AF.Ln, bias=epst[:, 0:1]),
             reads=[r_small], writes=[r_var])
        S.op("scalar", lambda e: e.activation(rst[:, :], var[:, :], AF.Exp, scale=-0.5),
             reads=[r_var], writes=[r_rst])
        S.op("vector", lambda e: e.tensor_mul(mu[:, :], mu[:, :], rst[:, :]),
             reads=[r_rst, r_mu], writes=[r_mu])


    def dump(nm, ap, res_list):
        if not debug:
            return
        S.op("gpsimd", (lambda nm=nm, ap=ap: lambda e: e.dma_start(
            out=ddump[nm][:, :], in_=ap))(), reads=res_list, writes=[r_yt], dma=True)
    def x_src(k):
        return ((lambda al=k % 2, k=k: lambda e: e.dma_start(
            out=xstg[al][:, :], in_=d_xt[k * 128:(k + 1) * 128, :]))(), None)

    stats_from(None, None, True, src_f32=x_src)
    for k in range(16):
        al = k % 2
        S.op("gpsimd", (lambda al=al, k=k: lambda e: e.dma_start(
            out=xstg[al][:, :], in_=d_xt[k * 128:(k + 1) * 128, :]))(),
            writes=[r_xstg[al]], dma=True)
        S.op("vector", (lambda al=al: lambda e: e.tensor_mul(tA[al][:, :], xstg[al][:, :], rst[:, :]))(),
             reads=[r_xstg[al], r_rst], writes=[r_tA[al]])
        S.op("vector", (lambda al=al, k=k: lambda e: e.tensor_sub(
            s1[:, k * TOK:(k + 1) * TOK], tA[al][:, :], mu[:, :]))(),
            reads=[r_tA[al], r_mu], writes=[r_s1])

    dump("dln1", s1[:, :], [r_s1])
    for m in range(48):
        al = m % 2
        S.op("sync", (lambda al=al, m=m: lambda e: e.dma_start(
            out=wstg[al][:, :], in_=d_wqkv[m]))(),
            writes=[r_wstg[al]], dma=True)
        S.op("vector", (lambda al=al: lambda e: e.tensor_copy(wbf[al][:, :], wstg[al][:, :]))(),
             reads=[r_wstg[al]], writes=[r_wbf[al]])

        def mm_qkv(e, al=al):
            i0 = None
            for k in range(16):
                for hhalf in range(2):
                    i0 = e.matmul(
                        (psA if al == 0 else psB)[:, hhalf * 512:(hhalf + 1) * 512],
                        wbf[al][:, k * 128:(k + 1) * 128],
                        s1[:, k * TOK + hhalf * 512: k * TOK + (hhalf + 1) * 512],
                        start=(k == 0), stop=(k == 15))
            return i0
        S.op("tensor", (lambda al=al: lambda e: mm_qkv(e, al))(),
             reads=[r_wbf[al], r_s1], writes=ps2[al])
        if m < 16:
            dst, rdst = s2[:, m::16], r_s2
        elif m < 32:
            dst, rdst = s3[:, (m - 16)::16], r_s3
        else:
            dst, rdst = s4[:, (m - 32) * TOK:(m - 31) * TOK], r_s4
        S.op("scalar", (lambda al=al, m=m, dst=dst: lambda e: e.activation(
            dst, (psA if al == 0 else psB)[:, :], AF.Identity, bias=bqkv[:, m:m + 1]))(),
            reads=ps2[al] + [r_small], writes=[rdst])

    dump("dqg", s2[:, :], [r_s2])
    dump("dkg", s3[:, :], [r_s3])
    dump("dvt", s4[:, :], [r_s4])
    for ts in range(8):
        for tq in range(4):
            al = (ts * 4 + tq) % 2
            pst = psT1 if al == 0 else psT2

            def mm_tr(e, ts=ts, tq=tq, pst=pst):
                i0 = None
                for i in range(4):
                    t = tq * 4 + i
                    i0 = e.transpose(pst[:, i * 128:(i + 1) * 128],
                                     s4[:, t * TOK + ts * 128: t * TOK + ts * 128 + 128],
                                     ident[:, :])
                return i0
            S.op("tensor", mm_tr, reads=[r_s4, r_small], writes=[r_psT[al]])
            S.op("vector", (lambda ts=ts, tq=tq, pst=pst: lambda e: e.tensor_copy(
                s1[:, ts * 2048 + tq * 512: ts * 2048 + (tq + 1) * 512], pst[:, :]))(),
                reads=[r_psT[al]], writes=[r_s1])

    dump("dvsb", s1[:, :], [r_s1])
    for G in range(128):
        al = G % 2
        psSsl = psC[:, 0:128] if al == 0 else psC[:, 512:640]
        r_psS = r_psCb0 if al == 0 else r_psCb1
        psPsl = psA[:, 0:129] if al == 0 else psA[:, 512:641]
        r_psP = r_psAb[al]
        S.op("gpsimd", (lambda G=G, al=al: lambda e: e.dma_start(
            out=vbg[al][:, 0:128],
            in_=s1[8 * (G & 15):8 * (G & 15) + 8,
                   (G >> 4) * 2048:((G >> 4) + 1) * 2048].rearrange(
                       "s (t d) -> s t d", t=16)))(),
            reads=[r_s1], writes=[r_vbg[al]], dma=True)
        S.op("tensor", (lambda G=G, psSsl=psSsl: lambda e: e.matmul(
            psSsl, s3[:, G * 128:(G + 1) * 128], s2[:, G * 128:(G + 1) * 128],
            start=True, stop=True))(),
            reads=[r_s3, r_s2], writes=[r_psS])
        S.op("scalar", (lambda al=al, psSsl=psSsl: lambda e: e.activation(
            ptg[al][:, :], psSsl, AF.Exp, scale=EXP_SCALE))(),
            reads=[r_psS], writes=[r_ptg[al]])
        S.op("vector", (lambda al=al: lambda e: e.tensor_mul(
            ptg[al][:, :], ptg[al][:, :], mask[:, :]))(),
            reads=[r_small], writes=[r_ptg[al]])
        S.op("tensor", (lambda al=al, psPsl=psPsl: lambda e: e.matmul(
            psPsl, ptg[al][:, :], vbg[al][:, 0:129], start=True, stop=True))(),
            reads=[r_ptg[al], r_vbg[al]], writes=[r_psP])
        S.op("vector", (lambda G=G, psPsl=psPsl: lambda e: e.tensor_copy(
            denall[:, G:G + 1], psPsl[:, 128:129]))(),
            reads=[r_psP], writes=[r_den])
        S.op("vector", (lambda G=G, psPsl=psPsl: lambda e: e.tensor_copy(
            s4[:, G * 128:(G + 1) * 128], psPsl[:, 0:128]))(),
            reads=[r_psP], writes=[r_s4])

    dump("dctx", s4[:, :], [r_s4])
    S.op("scalar", lambda e: e.activation(recal[:, :], denall[:, :], AF.Ln),
         reads=[r_den], writes=[r_recal])
    S.op("scalar", lambda e: e.activation(recal[:, :], recal[:, :], AF.Exp, scale=-1.0),
         reads=[], writes=[r_recal])
    for G in range(128):
        S.op("vector", (lambda G=G: lambda e: e.tensor_scalar(
            s4[:, G * 128:(G + 1) * 128], s4[:, G * 128:(G + 1) * 128],
            recal[:, G:G + 1], None, ALU.mult))(),
            sreads=[r_recal], writes=[r_s4])

    ctxT2v = s2.rearrange("p (j gh) -> p j gh", j=16)
    for G in range(128):
        al = G % 2
        pst = psT1 if al == 0 else psT2
        m_, par = G >> 1, G & 1
        S.op("tensor", (lambda G=G, pst=pst: lambda e: e.transpose(
            pst[:, 0:128], s4[:, G * 128:(G + 1) * 128], ident[:, :]))(),
            reads=[r_s4, r_small], writes=[r_psT[al]])
        S.op("vector", (lambda pst=pst, m_=m_, par=par: lambda e: e.tensor_copy(
            ctxT2v[:, 8 * par:8 * par + 8, m_ * 16:m_ * 16 + 16],
            pst[:, 0:128].rearrange("p (a h) -> p a h", a=8)))(),
            reads=[r_psT[al]], writes=[r_s2])

    dump("dctxT", s2[:, :], [r_s2])
    for m in range(16):
        al = m % 2
        S.op("sync", (lambda al=al, m=m: lambda e: e.dma_start(
            out=wstg[al][:, :], in_=d_wproj[m]))(),
            writes=[r_wstg[al]], dma=True)
        S.op("vector", (lambda al=al: lambda e: e.tensor_copy(wbf[al][:, :], wstg[al][:, :]))(),
             reads=[r_wstg[al]], writes=[r_wbf[al]])

        def mm_proj(e, al=al):
            i0 = None
            for j in range(16):
                for hhalf in range(2):
                    i0 = e.matmul(
                        (psA if al == 0 else psB)[:, hhalf * 512:(hhalf + 1) * 512],
                        wbf[al][:, j * 128:(j + 1) * 128],
                        s2[:, j * TOK + hhalf * 512: j * TOK + (hhalf + 1) * 512],
                        start=(j == 0), stop=(j == 15))
            return i0
        S.op("tensor", (lambda al=al: lambda e: mm_proj(e, al))(),
             reads=[r_wbf[al], r_s2], writes=ps2[al])
        S.op("scalar", (lambda al=al, m=m: lambda e: e.activation(
            s3[:, m * TOK:(m + 1) * TOK], (psA if al == 0 else psB)[:, :],
            AF.Identity, bias=bproj[:, m:m + 1]))(),
            reads=ps2[al] + [r_small], writes=[r_s3])

    dump("dattnT", s3[:, :], [r_s3])
    stats_from(lambda k: s3[:, k * TOK:(k + 1) * TOK], r_s3, False)
    for k in range(16):
        al = k % 2
        S.op("gpsimd", (lambda al=al, k=k: lambda e: e.dma_start(
            out=xstg[al][:, :], in_=d_xpt[k * 128:(k + 1) * 128, :]))(),
            writes=[r_xstg[al]], dma=True)
        S.op("vector", (lambda al=al, k=k: lambda e: e.tensor_mul(
            tA[al][:, :], s3[:, k * TOK:(k + 1) * TOK], rst[:, :]))(),
            reads=[r_s3, r_rst], writes=[r_tA[al]])
        S.op("vector", (lambda al=al: lambda e: e.tensor_sub(
            tA[al][:, :], tA[al][:, :], mu[:, :]))(),
            reads=[r_mu], writes=[r_tA[al]])
        S.op("vector", (lambda al=al: lambda e: e.tensor_add(
            tA[al][:, :], tA[al][:, :], xstg[al][:, :]))(),
            reads=[r_xstg[al]], writes=[r_tA[al]])
        S.op("gpsimd", (lambda al=al, k=k: lambda e: e.dma_start(
            out=d_x1[k * 128:(k + 1) * 128, :], in_=tA[al][:, :]))(),
            reads=[r_tA[al]], writes=[r_x1], dma=True)

    def x1_src(k):
        return ((lambda al=k % 2, k=k: lambda e: e.dma_start(
            out=xstg[al][:, :], in_=d_x1[k * 128:(k + 1) * 128, :]))(), r_x1)

    stats_from(None, None, True, src_f32=x1_src)
    for k in range(16):
        al = k % 2
        S.op("gpsimd", (lambda al=al, k=k: lambda e: e.dma_start(
            out=xstg[al][:, :], in_=d_x1[k * 128:(k + 1) * 128, :]))(),
            reads=[r_x1], writes=[r_xstg[al]], dma=True)
        S.op("vector", (lambda al=al: lambda e: e.tensor_mul(
            tA[al][:, :], xstg[al][:, :], rst[:, :]))(),
            reads=[r_xstg[al], r_rst], writes=[r_tA[al]])
        S.op("vector", (lambda al=al, k=k: lambda e: e.tensor_sub(
            s4[:, k * TOK:(k + 1) * TOK], tA[al][:, :], mu[:, :]))(),
            reads=[r_tA[al], r_mu], writes=[r_s4])

    dump("dbT", s4[:, :], [r_s4])
    if debug:
        S.op("gpsimd", lambda e: e.dma_start(out=ddump["dgbf"][:, :], in_=gbf[:, :]),
             reads=[r_small], writes=[r_yt], dma=True)
        S.op("gpsimd", lambda e: e.dma_start(out=ddump["dgstg"][:, :], in_=gstg[:, :]),
             reads=[r_small], writes=[r_yt], dma=True)
    if debug:
        S.op("gpsimd", lambda e: e.dma_start(out=ddump["dx1"][:, :], in_=d_x1[:, :]),
             reads=[r_x1], writes=[r_yt], dma=True)
    def mm_gateT(e):
        i0 = None
        for k in range(16):
            for hhalf in range(2):
                i0 = e.matmul(psC[0:8, hhalf * 512:(hhalf + 1) * 512],
                              gbf[:, k * 8:(k + 1) * 8],
                              s4[:, k * TOK + hhalf * 512: k * TOK + (hhalf + 1) * 512],
                              start=(k == 0), stop=(k == 15))
        return i0
    S.op("tensor", mm_gateT, reads=[r_s4, r_small], writes=[r_psCb0, r_psCb1])
    S.op("scalar", lambda e: e.copy(lgT[:, :], psC[0:8, :]),
         reads=[r_psCb0, r_psCb1], writes=[r_lgT])
    for tt in range(8):
        rb = r_psCb0 if tt % 2 == 0 else r_psCb1
        psl = psC[0:128, 0:8] if tt % 2 == 0 else psC[0:128, 512:520]
        S.op("tensor", (lambda tt=tt, psl=psl: lambda e: e.transpose(
            psl, lgT[0:8, tt * 128:(tt + 1) * 128], identf[0:8, 0:8]))(),
            reads=[r_lgT, r_small], writes=[rb])
        S.op("scalar", (lambda tt=tt, psl=psl: lambda e: e.copy(
            lgtB[:, tt * 8:(tt + 1) * 8], psl))(),
            reads=[rb], writes=[r_lgtB])

    if inject_lgt:
        S.op("gpsimd", lambda e: e.dma_start(out=lgtB[:, :], in_=d_lgtin[:, :]),
             reads=[r_lgtB], writes=[r_lgtB], dma=True)

    def pair_adds_all(e):
        i0 = None
        for tt in range(8):
            e.memset(s28B[:, tt * 32 + 28: tt * 32 + 32], -240.0)
        for tt in range(8):
            idx = 0
            for i in range(8):
                for j in range(i + 1, 8):
                    i0 = e.tensor_add(s28B[:, tt * 32 + idx: tt * 32 + idx + 1],
                                      lgtB[:, tt * 8 + i: tt * 8 + i + 1],
                                      lgtB[:, tt * 8 + j: tt * 8 + j + 1])
                    idx += 1
        return i0
    S.op("vector", lambda e: pair_adds_all(e), reads=[r_lgtB], writes=[r_s28B])
    for li, w in enumerate((16, 8, 4, 2, 1)):
        def lvl_subadd(e, w=w):
            i0 = None
            for tt in range(8):
                i0 = e.tensor_sub(s28dB[:, tt * 16: tt * 16 + w],
                                  s28B[:, tt * 32: tt * 32 + w],
                                  s28B[:, tt * 32 + w: tt * 32 + 2 * w])
                i0 = e.tensor_add(s28sB[:, tt * 16: tt * 16 + w],
                                  s28B[:, tt * 32: tt * 32 + w],
                                  s28B[:, tt * 32 + w: tt * 32 + 2 * w])
            return i0
        S.op("vector", (lambda w=w: lambda e: lvl_subadd(e, w))(),
             reads=[r_s28B], writes=[r_s28dB, r_s28sB])

        def lvl_abs(e, w=w):
            i0 = None
            for tt in range(8):
                i0 = e.activation(s28aB[:, tt * 16: tt * 16 + w],
                                  s28dB[:, tt * 16: tt * 16 + w], AF.Abs, scale=0.5)
            return i0
        S.op("scalar", (lambda w=w: lambda e: lvl_abs(e, w))(),
             reads=[r_s28dB], writes=[r_s28aB])
        if w > 1:
            def lvl_comb(e, w=w):
                i0 = None
                for tt in range(8):
                    i0 = e.scalar_tensor_tensor(
                        s28B[:, tt * 32: tt * 32 + w],
                        s28sB[:, tt * 16: tt * 16 + w], 0.5,
                        s28aB[:, tt * 16: tt * 16 + w], ALU.mult, ALU.add)
                return i0
            S.op("vector", (lambda w=w: lambda e: lvl_comb(e, w))(),
                 reads=[r_s28sB, r_s28aB], writes=[r_s28B])
        else:
            def lvl_fin(e):
                i0 = None
                for tt in range(8):
                    i0 = e.scalar_tensor_tensor(
                        sctok[:, tt:tt + 1], s28sB[:, tt * 16: tt * 16 + 1], 0.5,
                        s28aB[:, tt * 16: tt * 16 + 1], ALU.mult, ALU.add)
                return i0
            S.op("vector", lambda e: lvl_fin(e),
                 reads=[r_s28sB, r_s28aB], writes=[r_gate])

    if inject_sctok:
        S.op("gpsimd", lambda e: e.dma_start(out=sctok[:, :], in_=d_sctin[:, :]),
             reads=[r_gate], writes=[r_gate], dma=True)
    S.op("tensor", lambda e: e.transpose(psC[0:8, 512:640], sctok[:, :], identf[:, :]),
         reads=[r_gate, r_small], writes=[r_psCb1])
    S.op("scalar", lambda e: e.copy(scstT[:, :], psC[0:8, 512:640]),
         reads=[r_psCb1], writes=[r_gate])
    S.op("gpsimd", lambda e: e.dma_start(out=scT[0:1, :], in_=scstT[:, :]),
         reads=[r_gate], writes=[r_gate, r_scbc], dma=True)

    if debug:
        S.op("gpsimd", lambda e: e.dma_start(out=ddump["dscstT"][:, :], in_=scstT[:, :]),
             reads=[r_gate], writes=[r_yt], dma=True)

    def mm_scbc(e):
        i0 = None
        for hhalf in range(2):
            i0 = e.matmul(psC[:, hhalf * 512:(hhalf + 1) * 512], ones1f[0:1, :],
                          scT[0:1, hhalf * 512:(hhalf + 1) * 512],
                          start=True, stop=True)
        return i0
    S.op("tensor", mm_scbc, reads=[r_gate, r_small, r_scbc], writes=[r_psCb0, r_psCb1])
    S.op("scalar", lambda e: e.copy(scbc[:, :], psC[:, :]),
         reads=[r_psCb0, r_psCb1], writes=[r_scbc])

    if inject_scale:
        S.op("gpsimd", lambda e: e.dma_start(out=scbc[:, :], in_=d_scin[:, :]),
             reads=[r_scbc], writes=[r_scbc], dma=True)
    for m in range(16):
        al = m % 2
        S.op("sync", (lambda al=al, m=m: lambda e: e.dma_start(
            out=wstg[al][:, :], in_=d_wexp[m]))(),
            writes=[r_wstg[al]], dma=True)
        S.op("vector", (lambda al=al: lambda e: e.tensor_copy(wbf[al][:, :], wstg[al][:, :]))(),
             reads=[r_wstg[al]], writes=[r_wbf[al]])

        def mm_moe(e, al=al):
            i0 = None
            for k in range(16):
                for hhalf in range(2):
                    i0 = e.matmul(
                        (psA if al == 0 else psB)[:, hhalf * 512:(hhalf + 1) * 512],
                        wbf[al][:, k * 128:(k + 1) * 128],
                        s4[:, k * TOK + hhalf * 512: k * TOK + (hhalf + 1) * 512],
                        start=(k == 0), stop=(k == 15))
            return i0
        S.op("tensor", (lambda al=al: lambda e: mm_moe(e, al))(),
             reads=[r_wbf[al], r_s4], writes=ps2[al])
        S.op("vector", (lambda al=al, m=m: lambda e: e.tensor_mul(
            s1[:, m * TOK:(m + 1) * TOK], (psA if al == 0 else psB)[:, :], scbc[:, :]))(),
            reads=ps2[al] + [r_scbc], writes=[r_s1])

    for grp in range(4):
        for i in range(16):
            gi = grp * 16 + i
            al = gi % 2
            S.op("sync", (lambda al=al, gi=gi: lambda e: e.dma_start(
                out=wstg[al][:, :], in_=d_wup[gi]))(),
                writes=[r_wstg[al]], dma=True)
            S.op("vector", (lambda al=al: lambda e: e.tensor_copy(wbf[al][:, :], wstg[al][:, :]))(),
                 reads=[r_wstg[al]], writes=[r_wbf[al]])

            def mm_up(e, al=al):
                i0 = None
                for k in range(16):
                    for hhalf in range(2):
                        i0 = e.matmul(
                            (psA if al == 0 else psB)[:, hhalf * 512:(hhalf + 1) * 512],
                            wbf[al][:, k * 128:(k + 1) * 128],
                            s4[:, k * TOK + hhalf * 512: k * TOK + (hhalf + 1) * 512],
                            start=(k == 0), stop=(k == 15))
                return i0
            S.op("tensor", (lambda al=al: lambda e: mm_up(e, al))(),
                 reads=[r_wbf[al], r_s4], writes=ps2[al])
            S.op("vector", (lambda al=al: lambda e: e.tensor_scalar_max(
                rlb[al][:, :], (psA if al == 0 else psB)[:, :], 0.0))(),
                reads=ps2[al], writes=[r_rlb[al]])
            S.op("scalar", (lambda al=al, i=i: lambda e: e.square(
                s2[:, i * TOK:(i + 1) * TOK], rlb[al][:, :]))(),
                reads=[r_rlb[al]], writes=[r_s2])
        for m in range(16):
            al = m % 2
            S.op("sync", (lambda al=al, grp=grp, m=m: lambda e: e.dma_start(
                out=wstg[al][:, :],
                in_=d_wdown[grp * 16 + m]))(),
                writes=[r_wstg[al]], dma=True)
            S.op("vector", (lambda al=al: lambda e: e.tensor_copy(wbf[al][:, :], wstg[al][:, :]))(),
                 reads=[r_wstg[al]], writes=[r_wbf[al]])

            def mm_down(e, al=al):
                i0 = None
                for i2 in range(16):
                    for hhalf in range(2):
                        i0 = e.matmul(
                            psC[:, hhalf * 512:(hhalf + 1) * 512],
                            wbf[al][:, i2 * 128:(i2 + 1) * 128],
                            s2[:, i2 * TOK + hhalf * 512: i2 * TOK + (hhalf + 1) * 512],
                            start=(i2 == 0), stop=(i2 == 15))
                return i0
            S.op("tensor", (lambda al=al: lambda e: mm_down(e, al))(),
                 reads=[r_wbf[al], r_s2], writes=[r_psCb0, r_psCb1])
            S.op("vector", (lambda m=m: lambda e: e.tensor_add(
                s1[:, m * TOK:(m + 1) * TOK], s1[:, m * TOK:(m + 1) * TOK], psC[:, :]))(),
                reads=[r_psCb0, r_psCb1], writes=[r_s1])

    dump("dy", s1[:, :], [r_s1])
    dump("dscbc", scbc[:, :], [r_scbc])
    stats_from(lambda k: s1[:, k * TOK:(k + 1) * TOK], r_s1, False)
    for k in range(16):
        al = k % 2
        S.op("vector", (lambda al=al, k=k: lambda e: e.tensor_mul(
            tA[al][:, :], s1[:, k * TOK:(k + 1) * TOK], rst[:, :]))(),
            reads=[r_s1, r_rst], writes=[r_tA[al]])
        S.op("vector", (lambda al=al: lambda e: e.tensor_sub(
            tA[al][:, :], tA[al][:, :], mu[:, :]))(),
            reads=[r_mu], writes=[r_tA[al]])
        S.op("vector", (lambda al=al, k=k: lambda e: e.tensor_scalar(
            s3[:, k * TOK:(k + 1) * TOK], tA[al][:, :], gam[:, k:k + 1], bet[:, k:k + 1],
            ALU.mult, ALU.add))(),
            reads=[r_tA[al]], sreads=[r_small], writes=[r_s3])

    dump("df", s3[:, :], [r_s3])
    stats_from(lambda k: s3[:, k * TOK:(k + 1) * TOK], r_s3, False)
    for k in range(16):
        al = k % 2
        S.op("gpsimd", (lambda al=al, k=k: lambda e: e.dma_start(
            out=xstg[al][:, :], in_=d_x1[k * 128:(k + 1) * 128, :]))(),
            reads=[r_x1], writes=[r_xstg[al]], dma=True)
        S.op("vector", (lambda al=al, k=k: lambda e: e.tensor_mul(
            tA[al][:, :], s3[:, k * TOK:(k + 1) * TOK], rst[:, :]))(),
            reads=[r_s3, r_rst], writes=[r_tA[al]])
        S.op("vector", (lambda al=al: lambda e: e.tensor_sub(
            tA[al][:, :], tA[al][:, :], mu[:, :]))(),
            reads=[r_mu], writes=[r_tA[al]])
        S.op("vector", (lambda al=al: lambda e: e.tensor_add(
            tA[al][:, :], tA[al][:, :], xstg[al][:, :]))(),
            reads=[r_xstg[al]], writes=[r_tA[al]])
        S.op("gpsimd", (lambda al=al, k=k: lambda e: e.dma_start(
            out=d_yt[k * 128:(k + 1) * 128, :], in_=tA[al][:, :]))(),
            reads=[r_tA[al]], writes=[r_yt], dma=True)

    S.op("gpsimd", lambda e: e.nop(), reads=[r_yt])

    used = nc.SBUF_PARTITION_SIZE_BYTES - nc.sbuf_bytes_remaining
    assert used <= 224 * 1024, f"SBUF high-water {used} exceeds safe 224KB"

    nc._sched = S
    with nc.Block() as block:
        block.sync(S.replay("sync"))
        block.gpsimd(S.replay("gpsimd"))
        block.tensor(S.replay("tensor"))
        block.scalar(S.replay("scalar"))
        block.vector(S.replay("vector"))
    return nc


_NC_CACHE = None


def _get_nc():
    global _NC_CACHE
    if _NC_CACHE is None:
        _NC_CACHE = build_nc()
    return _NC_CACHE


def _tile_w(w):
    K, M = w.shape
    wt = w.reshape(16, 128, M).transpose(1, 0, 2)
    out = wt.reshape(128, 16, M // 128, 128).transpose(2, 0, 1, 3)
    return np.ascontiguousarray(out.reshape(M // 128, 128, 2048), dtype=np.float32)


def kernel(L0_attn_x, L0_attn_Wqkv, L0_attn_bqkv, L0_attn_Wproj, L0_attn_bproj,
           L0_ffn_gate_w, L0_ffn_up_w, L0_ffn_down_w, L0_ffn_expert_weight,
           L0_ffn_ln_gamma, L0_ffn_ln_beta):
    x = np.asarray(L0_attn_x, dtype=np.float32)[0]
    Wqkv = np.asarray(L0_attn_Wqkv, dtype=np.float32)
    bqkv = np.asarray(L0_attn_bqkv, dtype=np.float32)
    Wproj = np.asarray(L0_attn_Wproj, dtype=np.float32)
    bproj = np.asarray(L0_attn_bproj, dtype=np.float32)
    gate_w = np.asarray(L0_ffn_gate_w, dtype=np.float32)
    up_w = np.asarray(L0_ffn_up_w, dtype=np.float32)
    down_w = np.asarray(L0_ffn_down_w, dtype=np.float32)
    exp_w = np.asarray(L0_ffn_expert_weight, dtype=np.float32)
    gam = np.asarray(L0_ffn_ln_gamma, dtype=np.float32)
    bet = np.asarray(L0_ffn_ln_beta, dtype=np.float32)

    wqkv_t = _tile_w(Wqkv)
    wproj_t = _tile_w(Wproj)
    wup_t = _tile_w(up_w)
    wexp_t = _tile_w(exp_w)
    dn = down_w.reshape(4, 16, 128, 16, 128)
    wdown_t = np.ascontiguousarray(
        dn.transpose(0, 3, 2, 1, 4).reshape(64, 128, 2048), dtype=np.float32)
    bqkv_t = np.ascontiguousarray(bqkv.reshape(48, 128).T, dtype=np.float32)
    bproj_t = np.ascontiguousarray(bproj.reshape(16, 128).T, dtype=np.float32)
    gate_t = np.ascontiguousarray(
        gate_w.reshape(16, 128, 8).transpose(1, 0, 2).reshape(128, 128),
        dtype=np.float32)
    gam_t = np.ascontiguousarray(gam.reshape(16, 128).T, dtype=np.float32)
    bet_t = np.ascontiguousarray(bet.reshape(16, 128).T, dtype=np.float32)
    mask_t = np.kron(np.eye(8, dtype=np.float32),
                     np.ones((16, 16), np.float32)).astype(BF)
    ident_t = np.eye(128, dtype=np.float32).astype(BF)

    L = np.arange(1024)
    in_maps = []
    s2_rows = []
    for c in range(NCORE):
        s2s = (L & 15) * 512 + 64 * c + (L >> 4)
        s2_rows.append(s2s)
        in_maps.append({
            "xt": np.ascontiguousarray(x[c * 1024:(c + 1) * 1024, :].T),
            "xpt": np.ascontiguousarray(x[s2s, :].T),
            "wqkv": wqkv_t, "bqkv": bqkv_t,
            "wproj": wproj_t, "bproj": bproj_t,
            "gate": gate_t, "wup": wup_t, "wdown": wdown_t, "wexp": wexp_t,
            "gam": gam_t, "bet": bet_t,
            "mask": mask_t, "ident": ident_t,
        })

    nc = _get_nc()
    res = run_bass_kernel_spmd(nc, in_maps, core_ids=list(range(NCORE)))
    out = np.empty((1, 8192, 2048), dtype=np.float32)
    for c in range(NCORE):
        out[0, s2_rows[c], :] = res.results[c]["yt"].T
    return out


# version 14
# speedup vs baseline: 1.0204x; 1.0204x over previous
import numpy as np
import ml_dtypes
import concourse.bass as bass
import concourse.mybir as mybir
from concourse.bass_utils import run_bass_kernel_spmd

F32 = mybir.dt.float32
BF16 = mybir.dt.bfloat16
AF = mybir.ActivationFunctionType
ALU = mybir.AluOpType
BF = ml_dtypes.bfloat16

NCORE = 8
TOK = 1024
NK = 16
EXP_SCALE = 0.08838834764831845
LN_EPS = 1e-5

ENGINES = ("sync", "gpsimd", "tensor", "scalar", "vector")


class Res:
    __slots__ = ("w", "rd")

    def __init__(self):
        self.w = None
        self.rd = {}


class Sched:
    def __init__(self, nc, paranoid=False):
        self.nc = nc
        self.paranoid = paranoid
        self.cnt = {e: 0 for e in ENGINES}
        self.know = {e: {p: 0 for p in ENGINES} for e in ENGINES}
        self.self_done = {e: 0 for e in ENGINES}
        self.lastwait = {}
        self.snaps = {e: [] for e in ENGINES}
        self.ops = {e: [] for e in ENGINES}
        self.sem = {e: nc.alloc_semaphore("P_" + e) for e in ENGINES}

    def _need(self, eng, deps, is_dma):
        need = {}
        selfneed = 0
        for ref, force in deps:
            if ref is None:
                continue
            peng, pcnt, pdma = ref
            if peng == eng:
                if force or (pdma and not is_dma):
                    selfneed = max(selfneed, pcnt)
                continue
            if force:
                pass
            if need.get(peng, 0) < pcnt:
                need[peng] = pcnt
        waits = []
        k = self.know[eng]
        for peng, pcnt in sorted(need.items()):
            if k[peng] >= pcnt and not self.paranoid:
                continue
            if self.paranoid and self.lastwait.setdefault(eng, {}).get(peng, 0) >= pcnt:
                continue
            waits.append((self.sem[peng], pcnt))
            for (v, kn) in self.snaps[peng]:
                if v >= pcnt:
                    for e2, v2 in kn.items():
                        if k[e2] < v2:
                            k[e2] = v2
                    break
            if k[peng] < pcnt:
                k[peng] = pcnt
        if self.paranoid:
            for (s, v) in waits:
                for e2 in ENGINES:
                    if s is self.sem[e2]:
                        self.lastwait.setdefault(eng, {})[e2] = max(
                            self.lastwait.setdefault(eng, {}).get(e2, 0), v)
        if selfneed > self.self_done[eng]:
            waits.append((self.sem[eng], selfneed))
            self.self_done[eng] = selfneed
        return waits

    def op(self, eng, emit, reads=(), writes=(), sreads=(), dma=False, inc=None):
        deps = []
        for r in reads:
            deps.append((r.w, False))
        for r in sreads:
            deps.append((r.w, True))
        for r in writes:
            deps.append((r.w, False))
            for ref in r.rd.values():
                deps.append((ref, False))
        waits = self._need(eng, deps, dma)
        if inc is None:
            inc = 16 if dma else 1
        self.cnt[eng] += inc
        myref = (eng, self.cnt[eng], dma)
        self.know[eng][eng] = self.cnt[eng]
        self.snaps[eng].append((self.cnt[eng], dict(self.know[eng])))
        for r in reads:
            r.rd[eng] = myref
        for r in sreads:
            r.rd[eng] = myref
        for r in writes:
            r.w = myref
            r.rd = {}
        self.ops[eng].append((waits, emit, inc, self.sem[eng]))

    def replay(self, eng_name):
        sem_tbl = self.sem

        def body(e):
            for waits, emit, inc, sem in self.ops[eng_name]:
                for (s, v) in waits:
                    e.wait_ge(s, v)
                inst = emit(e)
                inst.then_inc(sem, inc)
        return body


def build_nc(debug=False, paranoid=False, inject_scale=False, inject_sctok=False, inject_lgt=False):
    nc = bass.Bass("TRN2", target_bir_lowering=False, dynamic_dma_scratch_size=8192)

    d_xt = nc.dram_tensor("xt", [2048, TOK], F32, kind="ExternalInput").ap()
    d_xpt = nc.dram_tensor("xpt", [2048, TOK], F32, kind="ExternalInput").ap()
    d_wqkv = nc.dram_tensor("wqkv", [48, 128, 2048], F32, kind="ExternalInput").ap()
    d_bqkv = nc.dram_tensor("bqkv", [128, 48], F32, kind="ExternalInput").ap()
    d_wproj = nc.dram_tensor("wproj", [16, 128, 2048], F32, kind="ExternalInput").ap()
    d_bproj = nc.dram_tensor("bproj", [128, 16], F32, kind="ExternalInput").ap()
    d_gate = nc.dram_tensor("gate", [128, 128], F32, kind="ExternalInput").ap()
    d_wup = nc.dram_tensor("wup", [64, 128, 2048], F32, kind="ExternalInput").ap()
    d_wdown = nc.dram_tensor("wdown", [64, 128, 2048], F32, kind="ExternalInput").ap()
    d_wexp = nc.dram_tensor("wexp", [16, 128, 2048], F32, kind="ExternalInput").ap()
    d_gam = nc.dram_tensor("gam", [128, 16], F32, kind="ExternalInput").ap()
    d_bet = nc.dram_tensor("bet", [128, 16], F32, kind="ExternalInput").ap()
    d_mask = nc.dram_tensor("mask", [128, 128], BF16, kind="ExternalInput").ap()
    d_ident = nc.dram_tensor("ident", [128, 128], BF16, kind="ExternalInput").ap()
    d_yt = nc.dram_tensor("yt", [2048, TOK], F32, kind="ExternalOutput").ap()
    d_scin = (nc.dram_tensor("scin", [128, TOK], F32, kind="ExternalInput").ap()
              if inject_scale else None)
    d_sctin = (nc.dram_tensor("sctin", [128, 8], F32, kind="ExternalInput").ap()
               if inject_sctok else None)
    d_lgtin = (nc.dram_tensor("lgtin", [128, 64], F32, kind="ExternalInput").ap()
               if inject_lgt else None)
    d_x1 = nc.dram_tensor("x1scr", [2048, TOK], F32).ap()
    ddump = {}
    if debug:
        for nm in ("dln1", "dqg", "dkg", "dvt", "dvsb", "dctx", "dctxT",
                   "dattnT", "dbT", "dy", "df"):
            ddump[nm] = nc.dram_tensor(nm, [128, 16 * TOK], BF16, kind="ExternalOutput").ap()
        ddump["dx1"] = nc.dram_tensor("dx1", [2048, TOK], F32, kind="ExternalOutput").ap()
        ddump["dscbc"] = nc.dram_tensor("dscbc", [128, TOK], F32, kind="ExternalOutput").ap()
        ddump["dsct"] = nc.dram_tensor("dsct", [128, 8], F32, kind="ExternalOutput").ap()
        ddump["dlgt"] = nc.dram_tensor("dlgt", [128, 8], F32, kind="ExternalOutput").ap()
        ddump["dscstT"] = nc.dram_tensor("dscstT", [8, 128], F32, kind="ExternalOutput").ap()
        ddump["dlgtall"] = nc.dram_tensor("dlgtall", [128, 64], F32, kind="ExternalOutput").ap()
        ddump["dgbf"] = nc.dram_tensor("dgbf", [128, 128], BF16, kind="ExternalOutput").ap()
        ddump["dgstg"] = nc.dram_tensor("dgstg", [128, 128], F32, kind="ExternalOutput").ap()
        ddump["dmx8all"] = nc.dram_tensor("dmx8all", [128, 64], F32, kind="ExternalOutput").ap()

    A = nc.alloc_sbuf_tensor
    s1 = A("s1", [128, 16 * TOK], BF16).ap()
    s2 = A("s2", [128, 16 * TOK], BF16).ap()
    s3 = A("s3", [128, 16 * TOK], BF16).ap()
    s4 = A("s4", [128, 16 * TOK], BF16).ap()
    mu = A("mu", [128, TOK], F32).ap()
    rst = A("rst", [128, TOK], F32).ap()
    tA = [A(f"tA{i}", [128, TOK], F32).ap() for i in range(2)]
    ex2 = tA[0]
    var = tA[1]
    xstg = [A(f"xstg{i}", [128, TOK], F32).ap() for i in range(2)]
    wstg = [A(f"wstg{i}", [128, 2048], F32).ap() for i in range(2)]
    scbc = A("scbc", [128, TOK], F32).ap()
    wbf = [A(f"wbf{i}", [128, 2048], BF16).ap() for i in range(2)]
    cbf = [A(f"cbf{i}", [128, TOK], BF16).ap() for i in range(2)]
    sqb = [A(f"sqb{i}", [128, TOK], BF16).ap() for i in range(2)]
    rlb = [A(f"rlb{i}", [128, TOK], BF16).ap() for i in range(2)]
    ptg = [A(f"ptg{i}", [128, 128], BF16).ap() for i in range(4)]
    vbg = [A(f"vbg{i}", [128, 132], BF16).ap() for i in range(4)]
    onesbf = A("onesbf", [128, 128], BF16).ap()
    mask = A("maskt", [128, 128], BF16).ap()
    ident = A("identt", [128, 128], BF16).ap()
    identf = A("identf", [128, 128], F32).ap()
    gbf = A("gbf", [128, 128], BF16).ap()
    bqkv = A("bqkvt", [128, 48], F32).ap()
    bproj = A("bprojt", [128, 16], F32).ap()
    gam = A("gamt", [128, 16], F32).ap()
    bet = A("bett", [128, 16], F32).ap()
    gstg = A("gstg", [128, 128], F32).ap()
    denall = A("denall", [128, 128], F32).ap()
    recal = A("recal", [128, 128], F32).ap()
    lgtB = A("lgtB", [128, 64], F32).ap()
    s28B = A("s28B", [128, 256], F32).ap()
    s28dB = A("s28dB", [128, 128], F32).ap()
    s28sB = A("s28sB", [128, 128], F32).ap()
    s28aB = A("s28aB", [128, 128], F32).ap()
    sctok = A("sctok", [128, 8], F32).ap()
    scstT = A("scstT", [8, 128], F32).ap()
    lgT = A("lgT", [8, TOK], F32).ap()
    scT = scbc[0:1, :]
    ones1f = A("ones1f", [1, 128], F32).ap()
    epst = A("epst", [128, 1], F32).ap()
    dly = A("dly", [128, 1], F32).ap()

    if debug:
        lgtall = A("lgtall", [128, 64], F32).ap()
        mx8all = A("mx8all", [128, 64], F32).ap()
        r_lgtall = Res(); r_mx8all = Res()
    P = nc.alloc_psum_tensor
    psA = P("psA", [128, TOK], F32).ap()
    psB = P("psB", [128, TOK], F32).ap()
    psC = P("psC", [128, TOK], F32).ap()
    psT1 = P("psT1", [128, 512], BF16).ap()
    psT2 = P("psT2", [128, 512], BF16).ap()

    S = Sched(nc, paranoid=paranoid)

    def RR():
        return Res()

    r_s1 = RR(); r_s2 = RR(); r_s3 = RR(); r_s4 = RR()
    r_mu = RR(); r_rst = RR()
    r_tA = [RR(), RR()]
    r_ex2 = r_tA[0]; r_var = r_tA[1]
    r_xstg = [RR(), RR()]; r_wstg = [RR(), RR()]
    r_wbf = [RR(), RR()]; r_cbf = [RR(), RR()]; r_sqb = [RR(), RR()]
    r_ptg4 = [RR(), RR(), RR(), RR()]; r_vbg4 = [RR(), RR(), RR(), RR()]
    r_rlb = r_cbf
    r_ptg = [RR(), RR()]; r_vbg = [RR(), RR()]
    r_scbc = RR(); r_small = RR()
    r_den = RR(); r_recal = RR()
    r_gate = RR()
    r_lgT = RR()
    r_lgtB = RR(); r_s28B = RR()
    r_s28dB = RR(); r_s28sB = RR(); r_s28aB = RR()
    r_psA0 = RR(); r_psA1 = RR()
    r_psB0 = RR(); r_psB1 = RR()
    r_psC0 = RR(); r_psC1 = RR()
    r_psA = [r_psA0, r_psA1]
    r_psB = [r_psB0, r_psB1]
    r_psCb0 = r_psC0; r_psCb1 = r_psC1
    r_psT = [RR(), RR()]
    r_x1 = RR()
    r_yt = RR()
    r_psAb = [r_psA0, r_psA1]

    ps2 = [r_psA, r_psB]

    S.op("sync", lambda e: e.dma_start(out=mask[:, :], in_=d_mask[:, :]), writes=[r_small], dma=True)
    S.op("sync", lambda e: e.dma_start(out=ident[:, :], in_=d_ident[:, :]), writes=[r_small], dma=True)
    S.op("sync", lambda e: e.dma_start(out=bqkv[:, :], in_=d_bqkv[:, :]), writes=[r_small], dma=True)
    S.op("sync", lambda e: e.dma_start(out=bproj[:, :], in_=d_bproj[:, :]), writes=[r_small], dma=True)
    S.op("sync", lambda e: e.dma_start(out=gam[:, :], in_=d_gam[:, :]), writes=[r_small], dma=True)
    S.op("sync", lambda e: e.dma_start(out=bet[:, :], in_=d_bet[:, :]), writes=[r_small], dma=True)
    S.op("sync", lambda e: e.dma_start(out=gstg[:, :], in_=d_gate[:, :]), writes=[r_small], dma=True)
    S.op("vector", lambda e: e.memset(onesbf[:, :], 1.0 / 2048.0), writes=[r_small])
    for _i in range(4):
        S.op("vector", (lambda _i=_i: lambda e: e.memset(vbg[_i][:, 128:129], 1.0))(),
             writes=[r_vbg4[_i]])
    S.op("vector", lambda e: e.memset(ones1f[:, :], 1.0), writes=[r_small])
    S.op("vector", lambda e: e.memset(epst[:, :], LN_EPS), writes=[r_small])
    S.op("vector", lambda e: e.tensor_copy(gbf[:, :], gstg[:, :]), reads=[r_small], writes=[r_small])
    S.op("scalar", lambda e: e.copy(identf[:, :], ident[:, :]), reads=[r_small], writes=[r_small])

    def stats_from(tiles_ap, tiles_res, cast_needed, src_f32=None, src_res=None):
        for k in range(16):
            al = k % 2
            if cast_needed:
                dma_emit, res_in = src_f32(k)
                S.op("gpsimd", dma_emit, reads=[res_in] if res_in else [], writes=[r_xstg[al]], dma=True)
                S.op("vector", (lambda al=al: lambda e: e.tensor_copy(cbf[al][:, :], xstg[al][:, :]))(),
                     reads=[r_xstg[al]], writes=[r_cbf[al]])
                S.op("scalar", (lambda al=al: lambda e: e.square(sqb[al][:, :], xstg[al][:, :]))(),
                     reads=[r_xstg[al]], writes=[r_sqb[al]])
                xa, xr = cbf[al], r_cbf[al]
            else:
                xa = tiles_ap(k)
                xr = tiles_res
                S.op("scalar", (lambda xa=xa, al=al: lambda e: e.square(sqb[al][:, :], xa))(),
                     reads=[xr], writes=[r_sqb[al]])
            def mm_sum(e, dst, src, k=k):
                i0 = None
                for hhalf in range(2):
                    i0 = e.matmul(dst[:, hhalf * 512:(hhalf + 1) * 512],
                                  onesbf[:, :], src[:, hhalf * 512:(hhalf + 1) * 512],
                                  start=(k == 0), stop=(k == 15))
                return i0
            S.op("tensor", (lambda xa=xa, k=k: lambda e: mm_sum(e, psA, xa, k))(),
                 reads=[xr, r_small], writes=r_psA)
            S.op("tensor", (lambda al=al, k=k: lambda e: mm_sum(e, psB, sqb[al][:, :], k))(),
                 reads=[r_sqb[al], r_small], writes=r_psB)
        S.op("scalar", lambda e: e.copy(mu[:, :], psA[:, :]), reads=r_psA, writes=[r_mu])
        S.op("scalar", lambda e: e.copy(ex2[:, :], psB[:, :]), reads=r_psB, writes=[r_ex2])
        S.op("vector", lambda e: e.tensor_mul(var[:, :], mu[:, :], mu[:, :]),
             reads=[r_mu], writes=[r_var])
        S.op("vector", lambda e: e.tensor_sub(var[:, :], ex2[:, :], var[:, :]),
             reads=[r_ex2, r_var], writes=[r_var])
        S.op("scalar", lambda e: e.activation(var[:, :], var[:, :], AF.Ln, bias=epst[:, 0:1]),
             reads=[r_small], writes=[r_var])
        S.op("scalar", lambda e: e.activation(rst[:, :], var[:, :], AF.Exp, scale=-0.5),
             reads=[r_var], writes=[r_rst])
        S.op("vector", lambda e: e.tensor_mul(mu[:, :], mu[:, :], rst[:, :]),
             reads=[r_rst, r_mu], writes=[r_mu])


    def dump(nm, ap, res_list):
        if not debug:
            return
        S.op("gpsimd", (lambda nm=nm, ap=ap: lambda e: e.dma_start(
            out=ddump[nm][:, :], in_=ap))(), reads=res_list, writes=[r_yt], dma=True)
    def x_src(k):
        return ((lambda al=k % 2, k=k: lambda e: e.dma_start(
            out=xstg[al][:, :], in_=d_xt[k * 128:(k + 1) * 128, :]))(), None)

    stats_from(None, None, True, src_f32=x_src)
    for k in range(16):
        al = k % 2
        S.op("gpsimd", (lambda al=al, k=k: lambda e: e.dma_start(
            out=xstg[al][:, :], in_=d_xt[k * 128:(k + 1) * 128, :]))(),
            writes=[r_xstg[al]], dma=True)
        S.op("vector", (lambda al=al: lambda e: e.tensor_mul(tA[al][:, :], xstg[al][:, :], rst[:, :]))(),
             reads=[r_xstg[al], r_rst], writes=[r_tA[al]])
        S.op("vector", (lambda al=al, k=k: lambda e: e.tensor_sub(
            s1[:, k * TOK:(k + 1) * TOK], tA[al][:, :], mu[:, :]))(),
            reads=[r_tA[al], r_mu], writes=[r_s1])

    dump("dln1", s1[:, :], [r_s1])
    for m in range(48):
        al = m % 2
        S.op("sync", (lambda al=al, m=m: lambda e: e.dma_start(
            out=wstg[al][:, :], in_=d_wqkv[m]))(),
            writes=[r_wstg[al]], dma=True)
        S.op("vector", (lambda al=al: lambda e: e.tensor_copy(wbf[al][:, :], wstg[al][:, :]))(),
             reads=[r_wstg[al]], writes=[r_wbf[al]])

        def mm_qkv(e, al=al):
            i0 = None
            for k in range(16):
                for hhalf in range(2):
                    i0 = e.matmul(
                        (psA if al == 0 else psB)[:, hhalf * 512:(hhalf + 1) * 512],
                        wbf[al][:, k * 128:(k + 1) * 128],
                        s1[:, k * TOK + hhalf * 512: k * TOK + (hhalf + 1) * 512],
                        start=(k == 0), stop=(k == 15))
            return i0
        S.op("tensor", (lambda al=al: lambda e: mm_qkv(e, al))(),
             reads=[r_wbf[al], r_s1], writes=ps2[al])
        if m < 16:
            dst, rdst = s2[:, m::16], r_s2
        elif m < 32:
            dst, rdst = s3[:, (m - 16)::16], r_s3
        else:
            dst, rdst = s4[:, (m - 32) * TOK:(m - 31) * TOK], r_s4
        S.op("scalar", (lambda al=al, m=m, dst=dst: lambda e: e.activation(
            dst, (psA if al == 0 else psB)[:, :], AF.Identity, bias=bqkv[:, m:m + 1]))(),
            reads=ps2[al] + [r_small], writes=[rdst])

    dump("dqg", s2[:, :], [r_s2])
    dump("dkg", s3[:, :], [r_s3])
    dump("dvt", s4[:, :], [r_s4])
    for ts in range(8):
        for tq in range(4):
            al = (ts * 4 + tq) % 2
            pst = psT1 if al == 0 else psT2

            def mm_tr(e, ts=ts, tq=tq, pst=pst):
                i0 = None
                for i in range(4):
                    t = tq * 4 + i
                    i0 = e.transpose(pst[:, i * 128:(i + 1) * 128],
                                     s4[:, t * TOK + ts * 128: t * TOK + ts * 128 + 128],
                                     ident[:, :])
                return i0
            S.op("tensor", mm_tr, reads=[r_s4, r_small], writes=[r_psT[al]])
            S.op("vector", (lambda ts=ts, tq=tq, pst=pst: lambda e: e.tensor_copy(
                s1[:, ts * 2048 + tq * 512: ts * 2048 + (tq + 1) * 512], pst[:, :]))(),
                reads=[r_psT[al]], writes=[r_s1])

    dump("dvsb", s1[:, :], [r_s1])
    mixS = [(psC[:, 0:128], r_psCb0), (psC[:, 512:640], r_psCb1),
            (psB[:, 0:128], r_psB0), (psB[:, 512:640], r_psB1)]
    for G in range(128):
        al = G % 2
        a4 = G % 4
        psSsl, r_psS = mixS[a4]
        psPsl = psA[:, 0:129] if al == 0 else psA[:, 512:641]
        r_psP = r_psAb[al]
        S.op("gpsimd", (lambda G=G, a4=a4: lambda e: e.dma_start(
            out=vbg[a4][:, 0:128],
            in_=s1[8 * (G & 15):8 * (G & 15) + 8,
                   (G >> 4) * 2048:((G >> 4) + 1) * 2048].rearrange(
                       "s (t d) -> s t d", t=16)))(),
            reads=[r_s1], writes=[r_vbg4[a4]], dma=True)
        S.op("tensor", (lambda G=G, psSsl=psSsl: lambda e: e.matmul(
            psSsl, s3[:, G * 128:(G + 1) * 128], s2[:, G * 128:(G + 1) * 128],
            start=True, stop=True))(),
            reads=[r_s3, r_s2], writes=[r_psS])
        S.op("scalar", (lambda a4=a4, psSsl=psSsl: lambda e: e.activation(
            ptg[a4][:, :], psSsl, AF.Exp, scale=EXP_SCALE))(),
            reads=[r_psS], writes=[r_ptg4[a4]])
        S.op("vector", (lambda a4=a4: lambda e: e.tensor_mul(
            ptg[a4][:, :], ptg[a4][:, :], mask[:, :]))(),
            reads=[r_small], writes=[r_ptg4[a4]])
        S.op("tensor", (lambda a4=a4, psPsl=psPsl: lambda e: e.matmul(
            psPsl, ptg[a4][:, :], vbg[a4][:, 0:129], start=True, stop=True))(),
            reads=[r_ptg4[a4], r_vbg4[a4]], writes=[r_psP])
        S.op("vector", (lambda G=G, psPsl=psPsl: lambda e: e.tensor_copy(
            denall[:, G:G + 1], psPsl[:, 128:129]))(),
            reads=[r_psP], writes=[r_den])
        S.op("vector", (lambda G=G, psPsl=psPsl: lambda e: e.tensor_copy(
            s4[:, G * 128:(G + 1) * 128], psPsl[:, 0:128]))(),
            reads=[r_psP], writes=[r_s4])

    dump("dctx", s4[:, :], [r_s4])
    S.op("scalar", lambda e: e.activation(recal[:, :], denall[:, :], AF.Ln),
         reads=[r_den], writes=[r_recal])
    S.op("scalar", lambda e: e.activation(recal[:, :], recal[:, :], AF.Exp, scale=-1.0),
         reads=[], writes=[r_recal])
    for G in range(128):
        S.op("vector", (lambda G=G: lambda e: e.tensor_scalar(
            s4[:, G * 128:(G + 1) * 128], s4[:, G * 128:(G + 1) * 128],
            recal[:, G:G + 1], None, ALU.mult))(),
            sreads=[r_recal], writes=[r_s4])

    ctxT2v = s2.rearrange("p (j gh) -> p j gh", j=16)
    for G in range(128):
        al = G % 2
        pst = psT1 if al == 0 else psT2
        m_, par = G >> 1, G & 1
        S.op("tensor", (lambda G=G, pst=pst: lambda e: e.transpose(
            pst[:, 0:128], s4[:, G * 128:(G + 1) * 128], ident[:, :]))(),
            reads=[r_s4, r_small], writes=[r_psT[al]])
        S.op("vector", (lambda pst=pst, m_=m_, par=par: lambda e: e.tensor_copy(
            ctxT2v[:, 8 * par:8 * par + 8, m_ * 16:m_ * 16 + 16],
            pst[:, 0:128].rearrange("p (a h) -> p a h", a=8)))(),
            reads=[r_psT[al]], writes=[r_s2])

    dump("dctxT", s2[:, :], [r_s2])
    for m in range(16):
        al = m % 2
        S.op("sync", (lambda al=al, m=m: lambda e: e.dma_start(
            out=wstg[al][:, :], in_=d_wproj[m]))(),
            writes=[r_wstg[al]], dma=True)
        S.op("vector", (lambda al=al: lambda e: e.tensor_copy(wbf[al][:, :], wstg[al][:, :]))(),
             reads=[r_wstg[al]], writes=[r_wbf[al]])

        def mm_proj(e, al=al):
            i0 = None
            for j in range(16):
                for hhalf in range(2):
                    i0 = e.matmul(
                        (psA if al == 0 else psB)[:, hhalf * 512:(hhalf + 1) * 512],
                        wbf[al][:, j * 128:(j + 1) * 128],
                        s2[:, j * TOK + hhalf * 512: j * TOK + (hhalf + 1) * 512],
                        start=(j == 0), stop=(j == 15))
            return i0
        S.op("tensor", (lambda al=al: lambda e: mm_proj(e, al))(),
             reads=[r_wbf[al], r_s2], writes=ps2[al])
        S.op("scalar", (lambda al=al, m=m: lambda e: e.activation(
            s3[:, m * TOK:(m + 1) * TOK], (psA if al == 0 else psB)[:, :],
            AF.Identity, bias=bproj[:, m:m + 1]))(),
            reads=ps2[al] + [r_small], writes=[r_s3])

    dump("dattnT", s3[:, :], [r_s3])
    stats_from(lambda k: s3[:, k * TOK:(k + 1) * TOK], r_s3, False)
    for k in range(16):
        al = k % 2
        S.op("gpsimd", (lambda al=al, k=k: lambda e: e.dma_start(
            out=xstg[al][:, :], in_=d_xpt[k * 128:(k + 1) * 128, :]))(),
            writes=[r_xstg[al]], dma=True)
        S.op("vector", (lambda al=al, k=k: lambda e: e.tensor_mul(
            tA[al][:, :], s3[:, k * TOK:(k + 1) * TOK], rst[:, :]))(),
            reads=[r_s3, r_rst], writes=[r_tA[al]])
        S.op("vector", (lambda al=al: lambda e: e.tensor_sub(
            tA[al][:, :], tA[al][:, :], mu[:, :]))(),
            reads=[r_mu], writes=[r_tA[al]])
        S.op("vector", (lambda al=al: lambda e: e.tensor_add(
            tA[al][:, :], tA[al][:, :], xstg[al][:, :]))(),
            reads=[r_xstg[al]], writes=[r_tA[al]])
        S.op("gpsimd", (lambda al=al, k=k: lambda e: e.dma_start(
            out=d_x1[k * 128:(k + 1) * 128, :], in_=tA[al][:, :]))(),
            reads=[r_tA[al]], writes=[r_x1], dma=True)

    def x1_src(k):
        return ((lambda al=k % 2, k=k: lambda e: e.dma_start(
            out=xstg[al][:, :], in_=d_x1[k * 128:(k + 1) * 128, :]))(), r_x1)

    stats_from(None, None, True, src_f32=x1_src)
    for k in range(16):
        al = k % 2
        S.op("gpsimd", (lambda al=al, k=k: lambda e: e.dma_start(
            out=xstg[al][:, :], in_=d_x1[k * 128:(k + 1) * 128, :]))(),
            reads=[r_x1], writes=[r_xstg[al]], dma=True)
        S.op("vector", (lambda al=al: lambda e: e.tensor_mul(
            tA[al][:, :], xstg[al][:, :], rst[:, :]))(),
            reads=[r_xstg[al], r_rst], writes=[r_tA[al]])
        S.op("vector", (lambda al=al, k=k: lambda e: e.tensor_sub(
            s4[:, k * TOK:(k + 1) * TOK], tA[al][:, :], mu[:, :]))(),
            reads=[r_tA[al], r_mu], writes=[r_s4])

    dump("dbT", s4[:, :], [r_s4])
    if debug:
        S.op("gpsimd", lambda e: e.dma_start(out=ddump["dgbf"][:, :], in_=gbf[:, :]),
             reads=[r_small], writes=[r_yt], dma=True)
        S.op("gpsimd", lambda e: e.dma_start(out=ddump["dgstg"][:, :], in_=gstg[:, :]),
             reads=[r_small], writes=[r_yt], dma=True)
    if debug:
        S.op("gpsimd", lambda e: e.dma_start(out=ddump["dx1"][:, :], in_=d_x1[:, :]),
             reads=[r_x1], writes=[r_yt], dma=True)
    def mm_gateT(e):
        i0 = None
        for k in range(16):
            for hhalf in range(2):
                i0 = e.matmul(psC[0:8, hhalf * 512:(hhalf + 1) * 512],
                              gbf[:, k * 8:(k + 1) * 8],
                              s4[:, k * TOK + hhalf * 512: k * TOK + (hhalf + 1) * 512],
                              start=(k == 0), stop=(k == 15))
        return i0
    S.op("tensor", mm_gateT, reads=[r_s4, r_small], writes=[r_psCb0, r_psCb1])
    S.op("scalar", lambda e: e.copy(lgT[:, :], psC[0:8, :]),
         reads=[r_psCb0, r_psCb1], writes=[r_lgT])
    for tt in range(8):
        rb = r_psCb0 if tt % 2 == 0 else r_psCb1
        psl = psC[0:128, 0:8] if tt % 2 == 0 else psC[0:128, 512:520]
        S.op("tensor", (lambda tt=tt, psl=psl: lambda e: e.transpose(
            psl, lgT[0:8, tt * 128:(tt + 1) * 128], identf[0:8, 0:8]))(),
            reads=[r_lgT, r_small], writes=[rb])
        S.op("scalar", (lambda tt=tt, psl=psl: lambda e: e.copy(
            lgtB[:, tt * 8:(tt + 1) * 8], psl))(),
            reads=[rb], writes=[r_lgtB])

    if inject_lgt:
        S.op("gpsimd", lambda e: e.dma_start(out=lgtB[:, :], in_=d_lgtin[:, :]),
             reads=[r_lgtB], writes=[r_lgtB], dma=True)

    def pair_adds_all(e):
        i0 = None
        for tt in range(8):
            e.memset(s28B[:, tt * 32 + 28: tt * 32 + 32], -240.0)
        for tt in range(8):
            idx = 0
            for i in range(8):
                for j in range(i + 1, 8):
                    i0 = e.tensor_add(s28B[:, tt * 32 + idx: tt * 32 + idx + 1],
                                      lgtB[:, tt * 8 + i: tt * 8 + i + 1],
                                      lgtB[:, tt * 8 + j: tt * 8 + j + 1])
                    idx += 1
        return i0
    S.op("vector", lambda e: pair_adds_all(e), reads=[r_lgtB], writes=[r_s28B])
    for li, w in enumerate((16, 8, 4, 2, 1)):
        def lvl_subadd(e, w=w):
            i0 = None
            for tt in range(8):
                i0 = e.tensor_sub(s28dB[:, tt * 16: tt * 16 + w],
                                  s28B[:, tt * 32: tt * 32 + w],
                                  s28B[:, tt * 32 + w: tt * 32 + 2 * w])
                i0 = e.tensor_add(s28sB[:, tt * 16: tt * 16 + w],
                                  s28B[:, tt * 32: tt * 32 + w],
                                  s28B[:, tt * 32 + w: tt * 32 + 2 * w])
            return i0
        S.op("vector", (lambda w=w: lambda e: lvl_subadd(e, w))(),
             reads=[r_s28B], writes=[r_s28dB, r_s28sB])

        def lvl_abs(e, w=w):
            i0 = None
            for tt in range(8):
                i0 = e.activation(s28aB[:, tt * 16: tt * 16 + w],
                                  s28dB[:, tt * 16: tt * 16 + w], AF.Abs, scale=0.5)
            return i0
        S.op("scalar", (lambda w=w: lambda e: lvl_abs(e, w))(),
             reads=[r_s28dB], writes=[r_s28aB])
        if w > 1:
            def lvl_comb(e, w=w):
                i0 = None
                for tt in range(8):
                    i0 = e.scalar_tensor_tensor(
                        s28B[:, tt * 32: tt * 32 + w],
                        s28sB[:, tt * 16: tt * 16 + w], 0.5,
                        s28aB[:, tt * 16: tt * 16 + w], ALU.mult, ALU.add)
                return i0
            S.op("vector", (lambda w=w: lambda e: lvl_comb(e, w))(),
                 reads=[r_s28sB, r_s28aB], writes=[r_s28B])
        else:
            def lvl_fin(e):
                i0 = None
                for tt in range(8):
                    i0 = e.scalar_tensor_tensor(
                        sctok[:, tt:tt + 1], s28sB[:, tt * 16: tt * 16 + 1], 0.5,
                        s28aB[:, tt * 16: tt * 16 + 1], ALU.mult, ALU.add)
                return i0
            S.op("vector", lambda e: lvl_fin(e),
                 reads=[r_s28sB, r_s28aB], writes=[r_gate])

    if inject_sctok:
        S.op("gpsimd", lambda e: e.dma_start(out=sctok[:, :], in_=d_sctin[:, :]),
             reads=[r_gate], writes=[r_gate], dma=True)
    S.op("tensor", lambda e: e.transpose(psC[0:8, 512:640], sctok[:, :], identf[:, :]),
         reads=[r_gate, r_small], writes=[r_psCb1])
    S.op("scalar", lambda e: e.copy(scstT[:, :], psC[0:8, 512:640]),
         reads=[r_psCb1], writes=[r_gate])
    S.op("gpsimd", lambda e: e.dma_start(out=scT[0:1, :], in_=scstT[:, :]),
         reads=[r_gate], writes=[r_gate, r_scbc], dma=True)

    if debug:
        S.op("gpsimd", lambda e: e.dma_start(out=ddump["dscstT"][:, :], in_=scstT[:, :]),
             reads=[r_gate], writes=[r_yt], dma=True)

    def mm_scbc(e):
        i0 = None
        for hhalf in range(2):
            i0 = e.matmul(psC[:, hhalf * 512:(hhalf + 1) * 512], ones1f[0:1, :],
                          scT[0:1, hhalf * 512:(hhalf + 1) * 512],
                          start=True, stop=True)
        return i0
    S.op("tensor", mm_scbc, reads=[r_gate, r_small, r_scbc], writes=[r_psCb0, r_psCb1])
    S.op("scalar", lambda e: e.copy(scbc[:, :], psC[:, :]),
         reads=[r_psCb0, r_psCb1], writes=[r_scbc])

    if inject_scale:
        S.op("gpsimd", lambda e: e.dma_start(out=scbc[:, :], in_=d_scin[:, :]),
             reads=[r_scbc], writes=[r_scbc], dma=True)
    for m in range(16):
        al = m % 2
        S.op("sync", (lambda al=al, m=m: lambda e: e.dma_start(
            out=wstg[al][:, :], in_=d_wexp[m]))(),
            writes=[r_wstg[al]], dma=True)
        S.op("vector", (lambda al=al: lambda e: e.tensor_copy(wbf[al][:, :], wstg[al][:, :]))(),
             reads=[r_wstg[al]], writes=[r_wbf[al]])

        def mm_moe(e, al=al):
            i0 = None
            for k in range(16):
                for hhalf in range(2):
                    i0 = e.matmul(
                        (psA if al == 0 else psB)[:, hhalf * 512:(hhalf + 1) * 512],
                        wbf[al][:, k * 128:(k + 1) * 128],
                        s4[:, k * TOK + hhalf * 512: k * TOK + (hhalf + 1) * 512],
                        start=(k == 0), stop=(k == 15))
            return i0
        S.op("tensor", (lambda al=al: lambda e: mm_moe(e, al))(),
             reads=[r_wbf[al], r_s4], writes=ps2[al])
        S.op("vector", (lambda al=al, m=m: lambda e: e.tensor_mul(
            s1[:, m * TOK:(m + 1) * TOK], (psA if al == 0 else psB)[:, :], scbc[:, :]))(),
            reads=ps2[al] + [r_scbc], writes=[r_s1])

    for grp in range(4):
        for i in range(16):
            gi = grp * 16 + i
            al = gi % 2
            S.op("sync", (lambda al=al, gi=gi: lambda e: e.dma_start(
                out=wstg[al][:, :], in_=d_wup[gi]))(),
                writes=[r_wstg[al]], dma=True)
            S.op("vector", (lambda al=al: lambda e: e.tensor_copy(wbf[al][:, :], wstg[al][:, :]))(),
                 reads=[r_wstg[al]], writes=[r_wbf[al]])

            def mm_up(e, al=al):
                i0 = None
                for k in range(16):
                    for hhalf in range(2):
                        i0 = e.matmul(
                            (psA if al == 0 else psB)[:, hhalf * 512:(hhalf + 1) * 512],
                            wbf[al][:, k * 128:(k + 1) * 128],
                            s4[:, k * TOK + hhalf * 512: k * TOK + (hhalf + 1) * 512],
                            start=(k == 0), stop=(k == 15))
                return i0
            S.op("tensor", (lambda al=al: lambda e: mm_up(e, al))(),
                 reads=[r_wbf[al], r_s4], writes=ps2[al])
            S.op("vector", (lambda al=al: lambda e: e.tensor_scalar_max(
                rlb[al][:, :], (psA if al == 0 else psB)[:, :], 0.0))(),
                reads=ps2[al], writes=[r_rlb[al]])
            S.op("scalar", (lambda al=al, i=i: lambda e: e.square(
                s2[:, i * TOK:(i + 1) * TOK], rlb[al][:, :]))(),
                reads=[r_rlb[al]], writes=[r_s2])
        for m in range(16):
            al = m % 2
            S.op("sync", (lambda al=al, grp=grp, m=m: lambda e: e.dma_start(
                out=wstg[al][:, :],
                in_=d_wdown[grp * 16 + m]))(),
                writes=[r_wstg[al]], dma=True)
            S.op("vector", (lambda al=al: lambda e: e.tensor_copy(wbf[al][:, :], wstg[al][:, :]))(),
                 reads=[r_wstg[al]], writes=[r_wbf[al]])

            def mm_down(e, al=al):
                i0 = None
                for i2 in range(16):
                    for hhalf in range(2):
                        i0 = e.matmul(
                            psC[:, hhalf * 512:(hhalf + 1) * 512],
                            wbf[al][:, i2 * 128:(i2 + 1) * 128],
                            s2[:, i2 * TOK + hhalf * 512: i2 * TOK + (hhalf + 1) * 512],
                            start=(i2 == 0), stop=(i2 == 15))
                return i0
            S.op("tensor", (lambda al=al: lambda e: mm_down(e, al))(),
                 reads=[r_wbf[al], r_s2], writes=[r_psCb0, r_psCb1])
            S.op("vector", (lambda m=m: lambda e: e.tensor_add(
                s1[:, m * TOK:(m + 1) * TOK], s1[:, m * TOK:(m + 1) * TOK], psC[:, :]))(),
                reads=[r_psCb0, r_psCb1], writes=[r_s1])

    dump("dy", s1[:, :], [r_s1])
    dump("dscbc", scbc[:, :], [r_scbc])
    stats_from(lambda k: s1[:, k * TOK:(k + 1) * TOK], r_s1, False)
    for k in range(16):
        al = k % 2
        S.op("vector", (lambda al=al, k=k: lambda e: e.tensor_mul(
            tA[al][:, :], s1[:, k * TOK:(k + 1) * TOK], rst[:, :]))(),
            reads=[r_s1, r_rst], writes=[r_tA[al]])
        S.op("vector", (lambda al=al: lambda e: e.tensor_sub(
            tA[al][:, :], tA[al][:, :], mu[:, :]))(),
            reads=[r_mu], writes=[r_tA[al]])
        S.op("vector", (lambda al=al, k=k: lambda e: e.tensor_scalar(
            s3[:, k * TOK:(k + 1) * TOK], tA[al][:, :], gam[:, k:k + 1], bet[:, k:k + 1],
            ALU.mult, ALU.add))(),
            reads=[r_tA[al]], sreads=[r_small], writes=[r_s3])

    dump("df", s3[:, :], [r_s3])
    stats_from(lambda k: s3[:, k * TOK:(k + 1) * TOK], r_s3, False)
    for k in range(16):
        al = k % 2
        S.op("gpsimd", (lambda al=al, k=k: lambda e: e.dma_start(
            out=xstg[al][:, :], in_=d_x1[k * 128:(k + 1) * 128, :]))(),
            reads=[r_x1], writes=[r_xstg[al]], dma=True)
        S.op("vector", (lambda al=al, k=k: lambda e: e.tensor_mul(
            tA[al][:, :], s3[:, k * TOK:(k + 1) * TOK], rst[:, :]))(),
            reads=[r_s3, r_rst], writes=[r_tA[al]])
        S.op("vector", (lambda al=al: lambda e: e.tensor_sub(
            tA[al][:, :], tA[al][:, :], mu[:, :]))(),
            reads=[r_mu], writes=[r_tA[al]])
        S.op("vector", (lambda al=al: lambda e: e.tensor_add(
            tA[al][:, :], tA[al][:, :], xstg[al][:, :]))(),
            reads=[r_xstg[al]], writes=[r_tA[al]])
        S.op("gpsimd", (lambda al=al, k=k: lambda e: e.dma_start(
            out=d_yt[k * 128:(k + 1) * 128, :], in_=tA[al][:, :]))(),
            reads=[r_tA[al]], writes=[r_yt], dma=True)

    S.op("gpsimd", lambda e: e.nop(), reads=[r_yt])

    used = nc.SBUF_PARTITION_SIZE_BYTES - nc.sbuf_bytes_remaining
    assert used <= 224 * 1024, f"SBUF high-water {used} exceeds safe 224KB"

    nc._sched = S
    with nc.Block() as block:
        block.sync(S.replay("sync"))
        block.gpsimd(S.replay("gpsimd"))
        block.tensor(S.replay("tensor"))
        block.scalar(S.replay("scalar"))
        block.vector(S.replay("vector"))
    return nc


_NC_CACHE = None


def _get_nc():
    global _NC_CACHE
    if _NC_CACHE is None:
        _NC_CACHE = build_nc()
    return _NC_CACHE


def _tile_w(w):
    K, M = w.shape
    wt = w.reshape(16, 128, M).transpose(1, 0, 2)
    out = wt.reshape(128, 16, M // 128, 128).transpose(2, 0, 1, 3)
    return np.ascontiguousarray(out.reshape(M // 128, 128, 2048), dtype=np.float32)


def kernel(L0_attn_x, L0_attn_Wqkv, L0_attn_bqkv, L0_attn_Wproj, L0_attn_bproj,
           L0_ffn_gate_w, L0_ffn_up_w, L0_ffn_down_w, L0_ffn_expert_weight,
           L0_ffn_ln_gamma, L0_ffn_ln_beta):
    x = np.asarray(L0_attn_x, dtype=np.float32)[0]
    Wqkv = np.asarray(L0_attn_Wqkv, dtype=np.float32)
    bqkv = np.asarray(L0_attn_bqkv, dtype=np.float32)
    Wproj = np.asarray(L0_attn_Wproj, dtype=np.float32)
    bproj = np.asarray(L0_attn_bproj, dtype=np.float32)
    gate_w = np.asarray(L0_ffn_gate_w, dtype=np.float32)
    up_w = np.asarray(L0_ffn_up_w, dtype=np.float32)
    down_w = np.asarray(L0_ffn_down_w, dtype=np.float32)
    exp_w = np.asarray(L0_ffn_expert_weight, dtype=np.float32)
    gam = np.asarray(L0_ffn_ln_gamma, dtype=np.float32)
    bet = np.asarray(L0_ffn_ln_beta, dtype=np.float32)

    wqkv_t = _tile_w(Wqkv)
    wproj_t = _tile_w(Wproj)
    wup_t = _tile_w(up_w)
    wexp_t = _tile_w(exp_w)
    dn = down_w.reshape(4, 16, 128, 16, 128)
    wdown_t = np.ascontiguousarray(
        dn.transpose(0, 3, 2, 1, 4).reshape(64, 128, 2048), dtype=np.float32)
    bqkv_t = np.ascontiguousarray(bqkv.reshape(48, 128).T, dtype=np.float32)
    bproj_t = np.ascontiguousarray(bproj.reshape(16, 128).T, dtype=np.float32)
    gate_t = np.ascontiguousarray(
        gate_w.reshape(16, 128, 8).transpose(1, 0, 2).reshape(128, 128),
        dtype=np.float32)
    gam_t = np.ascontiguousarray(gam.reshape(16, 128).T, dtype=np.float32)
    bet_t = np.ascontiguousarray(bet.reshape(16, 128).T, dtype=np.float32)
    mask_t = np.kron(np.eye(8, dtype=np.float32),
                     np.ones((16, 16), np.float32)).astype(BF)
    ident_t = np.eye(128, dtype=np.float32).astype(BF)

    L = np.arange(1024)
    in_maps = []
    s2_rows = []
    for c in range(NCORE):
        s2s = (L & 15) * 512 + 64 * c + (L >> 4)
        s2_rows.append(s2s)
        in_maps.append({
            "xt": np.ascontiguousarray(x[c * 1024:(c + 1) * 1024, :].T),
            "xpt": np.ascontiguousarray(x[s2s, :].T),
            "wqkv": wqkv_t, "bqkv": bqkv_t,
            "wproj": wproj_t, "bproj": bproj_t,
            "gate": gate_t, "wup": wup_t, "wdown": wdown_t, "wexp": wexp_t,
            "gam": gam_t, "bet": bet_t,
            "mask": mask_t, "ident": ident_t,
        })

    nc = _get_nc()
    res = run_bass_kernel_spmd(nc, in_maps, core_ids=list(range(NCORE)))
    out = np.empty((1, 8192, 2048), dtype=np.float32)
    for c in range(NCORE):
        out[0, s2_rows[c], :] = res.results[c]["yt"].T
    return out


# version 15 (final)
# speedup vs baseline: 1.0272x; 1.0066x over previous
import numpy as np
import ml_dtypes
import concourse.bass as bass
import concourse.mybir as mybir
from concourse.bass_utils import run_bass_kernel_spmd

F32 = mybir.dt.float32
BF16 = mybir.dt.bfloat16
AF = mybir.ActivationFunctionType
ALU = mybir.AluOpType
BF = ml_dtypes.bfloat16

NCORE = 8
TOK = 1024
NK = 16
EXP_SCALE = 0.08838834764831845
LN_EPS = 1e-5

ENGINES = ("sync", "gpsimd", "tensor", "scalar", "vector")


class Res:
    __slots__ = ("w", "rd")

    def __init__(self):
        self.w = None
        self.rd = {}


class Sched:
    def __init__(self, nc, paranoid=False):
        self.nc = nc
        self.paranoid = paranoid
        self.cnt = {e: 0 for e in ENGINES}
        self.know = {e: {p: 0 for p in ENGINES} for e in ENGINES}
        self.self_done = {e: 0 for e in ENGINES}
        self.lastwait = {}
        self.snaps = {e: [] for e in ENGINES}
        self.ops = {e: [] for e in ENGINES}
        self.sem = {e: nc.alloc_semaphore("P_" + e) for e in ENGINES}

    def _need(self, eng, deps, is_dma):
        need = {}
        selfneed = 0
        for ref, force in deps:
            if ref is None:
                continue
            peng, pcnt, pdma = ref
            if peng == eng:
                if force or (pdma and not is_dma):
                    selfneed = max(selfneed, pcnt)
                continue
            if force:
                pass
            if need.get(peng, 0) < pcnt:
                need[peng] = pcnt
        waits = []
        k = self.know[eng]
        for peng, pcnt in sorted(need.items()):
            if k[peng] >= pcnt and not self.paranoid:
                continue
            if self.paranoid and self.lastwait.setdefault(eng, {}).get(peng, 0) >= pcnt:
                continue
            waits.append((self.sem[peng], pcnt))
            for (v, kn) in self.snaps[peng]:
                if v >= pcnt:
                    for e2, v2 in kn.items():
                        if k[e2] < v2:
                            k[e2] = v2
                    break
            if k[peng] < pcnt:
                k[peng] = pcnt
        if self.paranoid:
            for (s, v) in waits:
                for e2 in ENGINES:
                    if s is self.sem[e2]:
                        self.lastwait.setdefault(eng, {})[e2] = max(
                            self.lastwait.setdefault(eng, {}).get(e2, 0), v)
        if selfneed > self.self_done[eng]:
            waits.append((self.sem[eng], selfneed))
            self.self_done[eng] = selfneed
        return waits

    def op(self, eng, emit, reads=(), writes=(), sreads=(), dma=False, inc=None):
        deps = []
        for r in reads:
            deps.append((r.w, False))
        for r in sreads:
            deps.append((r.w, True))
        for r in writes:
            deps.append((r.w, False))
            for ref in r.rd.values():
                deps.append((ref, False))
        waits = self._need(eng, deps, dma)
        if inc is None:
            inc = 16 if dma else 1
        self.cnt[eng] += inc
        myref = (eng, self.cnt[eng], dma)
        self.know[eng][eng] = self.cnt[eng]
        self.snaps[eng].append((self.cnt[eng], dict(self.know[eng])))
        for r in reads:
            r.rd[eng] = myref
        for r in sreads:
            r.rd[eng] = myref
        for r in writes:
            r.w = myref
            r.rd = {}
        self.ops[eng].append((waits, emit, inc, self.sem[eng]))

    def replay(self, eng_name):
        sem_tbl = self.sem

        def body(e):
            for waits, emit, inc, sem in self.ops[eng_name]:
                for (s, v) in waits:
                    e.wait_ge(s, v)
                inst = emit(e)
                inst.then_inc(sem, inc)
        return body


def build_nc(debug=False, paranoid=False, inject_scale=False, inject_sctok=False, inject_lgt=False):
    nc = bass.Bass("TRN2", target_bir_lowering=False, dynamic_dma_scratch_size=8192)

    d_xt = nc.dram_tensor("xt", [2048, TOK], F32, kind="ExternalInput").ap()
    d_xpt = nc.dram_tensor("xpt", [2048, TOK], F32, kind="ExternalInput").ap()
    d_wqkv = nc.dram_tensor("wqkv", [48, 128, 2048], F32, kind="ExternalInput").ap()
    d_bqkv = nc.dram_tensor("bqkv", [128, 48], F32, kind="ExternalInput").ap()
    d_wproj = nc.dram_tensor("wproj", [16, 128, 2048], F32, kind="ExternalInput").ap()
    d_bproj = nc.dram_tensor("bproj", [128, 16], F32, kind="ExternalInput").ap()
    d_gate = nc.dram_tensor("gate", [128, 128], F32, kind="ExternalInput").ap()
    d_wup = nc.dram_tensor("wup", [64, 128, 2048], F32, kind="ExternalInput").ap()
    d_wdown = nc.dram_tensor("wdown", [64, 128, 2048], F32, kind="ExternalInput").ap()
    d_wexp = nc.dram_tensor("wexp", [16, 128, 2048], F32, kind="ExternalInput").ap()
    d_gam = nc.dram_tensor("gam", [128, 16], F32, kind="ExternalInput").ap()
    d_bet = nc.dram_tensor("bet", [128, 16], F32, kind="ExternalInput").ap()
    d_mask = nc.dram_tensor("mask", [128, 128], BF16, kind="ExternalInput").ap()
    d_ident = nc.dram_tensor("ident", [128, 128], BF16, kind="ExternalInput").ap()
    d_yt = nc.dram_tensor("yt", [2048, TOK], F32, kind="ExternalOutput").ap()
    d_scin = (nc.dram_tensor("scin", [128, TOK], F32, kind="ExternalInput").ap()
              if inject_scale else None)
    d_sctin = (nc.dram_tensor("sctin", [128, 8], F32, kind="ExternalInput").ap()
               if inject_sctok else None)
    d_lgtin = (nc.dram_tensor("lgtin", [128, 64], F32, kind="ExternalInput").ap()
               if inject_lgt else None)
    d_x1 = nc.dram_tensor("x1scr", [2048, TOK], F32).ap()
    ddump = {}
    if debug:
        for nm in ("dln1", "dqg", "dkg", "dvt", "dvsb", "dctx", "dctxT",
                   "dattnT", "dbT", "dy", "df"):
            ddump[nm] = nc.dram_tensor(nm, [128, 16 * TOK], BF16, kind="ExternalOutput").ap()
        ddump["dx1"] = nc.dram_tensor("dx1", [2048, TOK], F32, kind="ExternalOutput").ap()
        ddump["dscbc"] = nc.dram_tensor("dscbc", [128, TOK], F32, kind="ExternalOutput").ap()
        ddump["dsct"] = nc.dram_tensor("dsct", [128, 8], F32, kind="ExternalOutput").ap()
        ddump["dlgt"] = nc.dram_tensor("dlgt", [128, 8], F32, kind="ExternalOutput").ap()
        ddump["dscstT"] = nc.dram_tensor("dscstT", [8, 128], F32, kind="ExternalOutput").ap()
        ddump["dlgtall"] = nc.dram_tensor("dlgtall", [128, 64], F32, kind="ExternalOutput").ap()
        ddump["dgbf"] = nc.dram_tensor("dgbf", [128, 128], BF16, kind="ExternalOutput").ap()
        ddump["dgstg"] = nc.dram_tensor("dgstg", [128, 128], F32, kind="ExternalOutput").ap()
        ddump["dmx8all"] = nc.dram_tensor("dmx8all", [128, 64], F32, kind="ExternalOutput").ap()

    A = nc.alloc_sbuf_tensor
    s1 = A("s1", [128, 16 * TOK], BF16).ap()
    s2 = A("s2", [128, 16 * TOK], BF16).ap()
    s3 = A("s3", [128, 16 * TOK], BF16).ap()
    s4 = A("s4", [128, 16 * TOK], BF16).ap()
    mu = A("mu", [128, TOK], F32).ap()
    rst = A("rst", [128, TOK], F32).ap()
    tA = [A(f"tA{i}", [128, TOK], F32).ap() for i in range(2)]
    ex2 = tA[0]
    var = tA[1]
    xstg = [A(f"xstg{i}", [128, TOK], F32).ap() for i in range(2)]
    wstg = [A(f"wstg{i}", [128, 2048], F32).ap() for i in range(3)]
    scbc = A("scbc", [128, TOK], F32).ap()
    wbf = [A(f"wbf{i}", [128, 2048], BF16).ap() for i in range(2)]
    cbf = [A(f"cbf{i}", [128, TOK], BF16).ap() for i in range(2)]
    sqb = [A(f"sqb{i}", [128, TOK], BF16).ap() for i in range(2)]
    rlb = [A(f"rlb{i}", [128, TOK], BF16).ap() for i in range(2)]
    ptg = [A(f"ptg{i}", [128, 128], BF16).ap() for i in range(4)]
    vbg = [A(f"vbg{i}", [128, 132], BF16).ap() for i in range(4)]
    onesbf = A("onesbf", [128, 128], BF16).ap()
    mask = A("maskt", [128, 128], BF16).ap()
    ident = A("identt", [128, 128], BF16).ap()
    identf = A("identf", [128, 128], F32).ap()
    gbf = A("gbf", [128, 128], BF16).ap()
    bqkv = A("bqkvt", [128, 48], F32).ap()
    bproj = A("bprojt", [128, 16], F32).ap()
    gam = A("gamt", [128, 16], F32).ap()
    bet = A("bett", [128, 16], F32).ap()
    gstg = A("gstg", [128, 128], F32).ap()
    denall = A("denall", [128, 128], F32).ap()
    recal = A("recal", [128, 128], F32).ap()
    lgtB = A("lgtB", [128, 64], F32).ap()
    s28B = A("s28B", [128, 256], F32).ap()
    s28dB = A("s28dB", [128, 128], F32).ap()
    s28sB = A("s28sB", [128, 128], F32).ap()
    s28aB = A("s28aB", [128, 128], F32).ap()
    sctok = A("sctok", [128, 8], F32).ap()
    scstT = A("scstT", [8, 128], F32).ap()
    lgT = A("lgT", [8, TOK], F32).ap()
    scT = scbc[0:1, :]
    ones1f = A("ones1f", [1, 128], F32).ap()
    epst = A("epst", [128, 1], F32).ap()
    dly = A("dly", [128, 1], F32).ap()

    if debug:
        lgtall = A("lgtall", [128, 64], F32).ap()
        mx8all = A("mx8all", [128, 64], F32).ap()
        r_lgtall = Res(); r_mx8all = Res()
    P = nc.alloc_psum_tensor
    psA = P("psA", [128, TOK], F32).ap()
    psB = P("psB", [128, TOK], F32).ap()
    psC = P("psC", [128, TOK], F32).ap()
    psT1 = P("psT1", [128, 512], BF16).ap()
    psT2 = P("psT2", [128, 512], BF16).ap()

    S = Sched(nc, paranoid=paranoid)

    def RR():
        return Res()

    r_s1 = RR(); r_s2 = RR(); r_s3 = RR(); r_s4 = RR()
    r_mu = RR(); r_rst = RR()
    r_tA = [RR(), RR()]
    r_ex2 = r_tA[0]; r_var = r_tA[1]
    r_xstg = [RR(), RR()]; r_wstg = [RR(), RR(), RR()]
    r_wbf = [RR(), RR()]; r_cbf = [RR(), RR()]; r_sqb = [RR(), RR()]
    r_ptg4 = [RR(), RR(), RR(), RR()]; r_vbg4 = [RR(), RR(), RR(), RR()]
    r_rlb = r_cbf
    r_ptg = [RR(), RR()]; r_vbg = [RR(), RR()]
    r_scbc = RR(); r_small = RR()
    r_den = RR(); r_recal = RR()
    r_gate = RR()
    r_lgT = RR()
    r_lgtB = RR(); r_s28B = RR()
    r_s28dB = RR(); r_s28sB = RR(); r_s28aB = RR()
    r_psA0 = RR(); r_psA1 = RR()
    r_psB0 = RR(); r_psB1 = RR()
    r_psC0 = RR(); r_psC1 = RR()
    r_psA = [r_psA0, r_psA1]
    r_psB = [r_psB0, r_psB1]
    r_psCb0 = r_psC0; r_psCb1 = r_psC1
    r_psT = [RR(), RR()]
    r_x1 = RR()
    r_yt = RR()
    r_psAb = [r_psA0, r_psA1]

    ps2 = [r_psA, r_psB]

    S.op("sync", lambda e: e.dma_start(out=mask[:, :], in_=d_mask[:, :]), writes=[r_small], dma=True)
    S.op("sync", lambda e: e.dma_start(out=ident[:, :], in_=d_ident[:, :]), writes=[r_small], dma=True)
    S.op("sync", lambda e: e.dma_start(out=bqkv[:, :], in_=d_bqkv[:, :]), writes=[r_small], dma=True)
    S.op("sync", lambda e: e.dma_start(out=bproj[:, :], in_=d_bproj[:, :]), writes=[r_small], dma=True)
    S.op("sync", lambda e: e.dma_start(out=gam[:, :], in_=d_gam[:, :]), writes=[r_small], dma=True)
    S.op("sync", lambda e: e.dma_start(out=bet[:, :], in_=d_bet[:, :]), writes=[r_small], dma=True)
    S.op("sync", lambda e: e.dma_start(out=gstg[:, :], in_=d_gate[:, :]), writes=[r_small], dma=True)
    S.op("vector", lambda e: e.memset(onesbf[:, :], 1.0 / 2048.0), writes=[r_small])
    for _i in range(4):
        S.op("vector", (lambda _i=_i: lambda e: e.memset(vbg[_i][:, 128:129], 1.0))(),
             writes=[r_vbg4[_i]])
    S.op("vector", lambda e: e.memset(ones1f[:, :], 1.0), writes=[r_small])
    S.op("vector", lambda e: e.memset(epst[:, :], LN_EPS), writes=[r_small])
    S.op("vector", lambda e: e.tensor_copy(gbf[:, :], gstg[:, :]), reads=[r_small], writes=[r_small])
    S.op("scalar", lambda e: e.copy(identf[:, :], ident[:, :]), reads=[r_small], writes=[r_small])

    def stats_from(tiles_ap, tiles_res, cast_needed, src_f32=None, src_res=None):
        for k in range(16):
            al = k % 2
            if cast_needed:
                dma_emit, res_in = src_f32(k)
                S.op("gpsimd", dma_emit, reads=[res_in] if res_in else [], writes=[r_xstg[al]], dma=True)
                S.op("vector", (lambda al=al: lambda e: e.tensor_copy(cbf[al][:, :], xstg[al][:, :]))(),
                     reads=[r_xstg[al]], writes=[r_cbf[al]])
                S.op("scalar", (lambda al=al: lambda e: e.square(sqb[al][:, :], xstg[al][:, :]))(),
                     reads=[r_xstg[al]], writes=[r_sqb[al]])
                xa, xr = cbf[al], r_cbf[al]
            else:
                xa = tiles_ap(k)
                xr = tiles_res
                S.op("scalar", (lambda xa=xa, al=al: lambda e: e.square(sqb[al][:, :], xa))(),
                     reads=[xr], writes=[r_sqb[al]])
            def mm_sum(e, dst, src, k=k):
                i0 = None
                for hhalf in range(2):
                    i0 = e.matmul(dst[:, hhalf * 512:(hhalf + 1) * 512],
                                  onesbf[:, :], src[:, hhalf * 512:(hhalf + 1) * 512],
                                  start=(k == 0), stop=(k == 15))
                return i0
            S.op("tensor", (lambda xa=xa, k=k: lambda e: mm_sum(e, psA, xa, k))(),
                 reads=[xr, r_small], writes=r_psA)
            S.op("tensor", (lambda al=al, k=k: lambda e: mm_sum(e, psB, sqb[al][:, :], k))(),
                 reads=[r_sqb[al], r_small], writes=r_psB)
        S.op("scalar", lambda e: e.copy(mu[:, :], psA[:, :]), reads=r_psA, writes=[r_mu])
        S.op("scalar", lambda e: e.copy(ex2[:, :], psB[:, :]), reads=r_psB, writes=[r_ex2])
        S.op("vector", lambda e: e.tensor_mul(var[:, :], mu[:, :], mu[:, :]),
             reads=[r_mu], writes=[r_var])
        S.op("vector", lambda e: e.tensor_sub(var[:, :], ex2[:, :], var[:, :]),
             reads=[r_ex2, r_var], writes=[r_var])
        S.op("scalar", lambda e: e.activation(var[:, :], var[:, :], AF.Ln, bias=epst[:, 0:1]),
             reads=[r_small], writes=[r_var])
        S.op("scalar", lambda e: e.activation(rst[:, :], var[:, :], AF.Exp, scale=-0.5),
             reads=[r_var], writes=[r_rst])
        S.op("vector", lambda e: e.tensor_mul(mu[:, :], mu[:, :], rst[:, :]),
             reads=[r_rst, r_mu], writes=[r_mu])


    def dump(nm, ap, res_list):
        if not debug:
            return
        S.op("gpsimd", (lambda nm=nm, ap=ap: lambda e: e.dma_start(
            out=ddump[nm][:, :], in_=ap))(), reads=res_list, writes=[r_yt], dma=True)
    def x_src(k):
        return ((lambda al=k % 2, k=k: lambda e: e.dma_start(
            out=xstg[al][:, :], in_=d_xt[k * 128:(k + 1) * 128, :]))(), None)

    stats_from(None, None, True, src_f32=x_src)
    for k in range(16):
        al = k % 2
        S.op("gpsimd", (lambda al=al, k=k: lambda e: e.dma_start(
            out=xstg[al][:, :], in_=d_xt[k * 128:(k + 1) * 128, :]))(),
            writes=[r_xstg[al]], dma=True)
        S.op("vector", (lambda al=al: lambda e: e.tensor_mul(tA[al][:, :], xstg[al][:, :], rst[:, :]))(),
             reads=[r_xstg[al], r_rst], writes=[r_tA[al]])
        S.op("vector", (lambda al=al, k=k: lambda e: e.tensor_sub(
            s1[:, k * TOK:(k + 1) * TOK], tA[al][:, :], mu[:, :]))(),
            reads=[r_tA[al], r_mu], writes=[r_s1])

    dump("dln1", s1[:, :], [r_s1])
    for m in range(48):
        al = m % 2
        aw = m % 3
        S.op("sync", (lambda aw=aw, m=m: lambda e: e.dma_start(
            out=wstg[aw][:, :], in_=d_wqkv[m]))(),
            writes=[r_wstg[aw]], dma=True)
        S.op("vector", (lambda al=al, aw=aw: lambda e: e.tensor_copy(wbf[al][:, :], wstg[aw][:, :]))(),
             reads=[r_wstg[aw]], writes=[r_wbf[al]])

        def mm_qkv(e, al=al):
            i0 = None
            for k in range(16):
                for hhalf in range(2):
                    i0 = e.matmul(
                        (psA if al == 0 else psB)[:, hhalf * 512:(hhalf + 1) * 512],
                        wbf[al][:, k * 128:(k + 1) * 128],
                        s1[:, k * TOK + hhalf * 512: k * TOK + (hhalf + 1) * 512],
                        start=(k == 0), stop=(k == 15))
            return i0
        S.op("tensor", (lambda al=al: lambda e: mm_qkv(e, al))(),
             reads=[r_wbf[al], r_s1], writes=ps2[al])
        if m < 16:
            dst, rdst = s2[:, m::16], r_s2
        elif m < 32:
            dst, rdst = s3[:, (m - 16)::16], r_s3
        else:
            dst, rdst = s4[:, (m - 32) * TOK:(m - 31) * TOK], r_s4
        S.op("scalar", (lambda al=al, m=m, dst=dst: lambda e: e.activation(
            dst, (psA if al == 0 else psB)[:, :], AF.Identity, bias=bqkv[:, m:m + 1]))(),
            reads=ps2[al] + [r_small], writes=[rdst])

    dump("dqg", s2[:, :], [r_s2])
    dump("dkg", s3[:, :], [r_s3])
    dump("dvt", s4[:, :], [r_s4])
    for ts in range(8):
        for tq in range(4):
            al = (ts * 4 + tq) % 2
            pst = psT1 if al == 0 else psT2

            def mm_tr(e, ts=ts, tq=tq, pst=pst):
                i0 = None
                for i in range(4):
                    t = tq * 4 + i
                    i0 = e.transpose(pst[:, i * 128:(i + 1) * 128],
                                     s4[:, t * TOK + ts * 128: t * TOK + ts * 128 + 128],
                                     ident[:, :])
                return i0
            S.op("tensor", mm_tr, reads=[r_s4, r_small], writes=[r_psT[al]])
            S.op("vector", (lambda ts=ts, tq=tq, pst=pst: lambda e: e.tensor_copy(
                s1[:, ts * 2048 + tq * 512: ts * 2048 + (tq + 1) * 512], pst[:, :]))(),
                reads=[r_psT[al]], writes=[r_s1])

    dump("dvsb", s1[:, :], [r_s1])
    mixS = [(psC[:, 0:128], r_psCb0), (psC[:, 512:640], r_psCb1),
            (psB[:, 0:128], r_psB0), (psB[:, 512:640], r_psB1)]
    for G in range(128):
        al = G % 2
        a4 = G % 4
        psSsl, r_psS = mixS[a4]
        psPsl = psA[:, 0:129] if al == 0 else psA[:, 512:641]
        r_psP = r_psAb[al]
        S.op("gpsimd", (lambda G=G, a4=a4: lambda e: e.dma_start(
            out=vbg[a4][:, 0:128],
            in_=s1[8 * (G & 15):8 * (G & 15) + 8,
                   (G >> 4) * 2048:((G >> 4) + 1) * 2048].rearrange(
                       "s (t d) -> s t d", t=16)))(),
            reads=[r_s1], writes=[r_vbg4[a4]], dma=True)
        S.op("tensor", (lambda G=G, psSsl=psSsl: lambda e: e.matmul(
            psSsl, s3[:, G * 128:(G + 1) * 128], s2[:, G * 128:(G + 1) * 128],
            start=True, stop=True))(),
            reads=[r_s3, r_s2], writes=[r_psS])
        S.op("scalar", (lambda a4=a4, psSsl=psSsl: lambda e: e.activation(
            ptg[a4][:, :], psSsl, AF.Exp, scale=EXP_SCALE))(),
            reads=[r_psS], writes=[r_ptg4[a4]])
        S.op("vector", (lambda a4=a4: lambda e: e.tensor_mul(
            ptg[a4][:, :], ptg[a4][:, :], mask[:, :]))(),
            reads=[r_small], writes=[r_ptg4[a4]])
        S.op("tensor", (lambda a4=a4, psPsl=psPsl: lambda e: e.matmul(
            psPsl, ptg[a4][:, :], vbg[a4][:, 0:129], start=True, stop=True))(),
            reads=[r_ptg4[a4], r_vbg4[a4]], writes=[r_psP])
        S.op("vector", (lambda G=G, psPsl=psPsl: lambda e: e.tensor_copy(
            denall[:, G:G + 1], psPsl[:, 128:129]))(),
            reads=[r_psP], writes=[r_den])
        S.op("vector", (lambda G=G, psPsl=psPsl: lambda e: e.tensor_copy(
            s4[:, G * 128:(G + 1) * 128], psPsl[:, 0:128]))(),
            reads=[r_psP], writes=[r_s4])

    dump("dctx", s4[:, :], [r_s4])
    S.op("scalar", lambda e: e.activation(recal[:, :], denall[:, :], AF.Ln),
         reads=[r_den], writes=[r_recal])
    S.op("scalar", lambda e: e.activation(recal[:, :], recal[:, :], AF.Exp, scale=-1.0),
         reads=[], writes=[r_recal])
    for G in range(128):
        S.op("vector", (lambda G=G: lambda e: e.tensor_scalar(
            s4[:, G * 128:(G + 1) * 128], s4[:, G * 128:(G + 1) * 128],
            recal[:, G:G + 1], None, ALU.mult))(),
            sreads=[r_recal], writes=[r_s4])

    ctxT2v = s2.rearrange("p (j gh) -> p j gh", j=16)
    for G in range(128):
        al = G % 2
        pst = psT1 if al == 0 else psT2
        m_, par = G >> 1, G & 1
        S.op("tensor", (lambda G=G, pst=pst: lambda e: e.transpose(
            pst[:, 0:128], s4[:, G * 128:(G + 1) * 128], ident[:, :]))(),
            reads=[r_s4, r_small], writes=[r_psT[al]])
        S.op("vector", (lambda pst=pst, m_=m_, par=par: lambda e: e.tensor_copy(
            ctxT2v[:, 8 * par:8 * par + 8, m_ * 16:m_ * 16 + 16],
            pst[:, 0:128].rearrange("p (a h) -> p a h", a=8)))(),
            reads=[r_psT[al]], writes=[r_s2])

    dump("dctxT", s2[:, :], [r_s2])
    for m in range(16):
        al = m % 2
        aw = m % 3
        S.op("sync", (lambda aw=aw, m=m: lambda e: e.dma_start(
            out=wstg[aw][:, :], in_=d_wproj[m]))(),
            writes=[r_wstg[aw]], dma=True)
        S.op("vector", (lambda al=al, aw=aw: lambda e: e.tensor_copy(wbf[al][:, :], wstg[aw][:, :]))(),
             reads=[r_wstg[aw]], writes=[r_wbf[al]])

        def mm_proj(e, al=al):
            i0 = None
            for j in range(16):
                for hhalf in range(2):
                    i0 = e.matmul(
                        (psA if al == 0 else psB)[:, hhalf * 512:(hhalf + 1) * 512],
                        wbf[al][:, j * 128:(j + 1) * 128],
                        s2[:, j * TOK + hhalf * 512: j * TOK + (hhalf + 1) * 512],
                        start=(j == 0), stop=(j == 15))
            return i0
        S.op("tensor", (lambda al=al: lambda e: mm_proj(e, al))(),
             reads=[r_wbf[al], r_s2], writes=ps2[al])
        S.op("scalar", (lambda al=al, m=m: lambda e: e.activation(
            s3[:, m * TOK:(m + 1) * TOK], (psA if al == 0 else psB)[:, :],
            AF.Identity, bias=bproj[:, m:m + 1]))(),
            reads=ps2[al] + [r_small], writes=[r_s3])

    dump("dattnT", s3[:, :], [r_s3])
    stats_from(lambda k: s3[:, k * TOK:(k + 1) * TOK], r_s3, False)
    for k in range(16):
        al = k % 2
        S.op("gpsimd", (lambda al=al, k=k: lambda e: e.dma_start(
            out=xstg[al][:, :], in_=d_xpt[k * 128:(k + 1) * 128, :]))(),
            writes=[r_xstg[al]], dma=True)
        S.op("vector", (lambda al=al, k=k: lambda e: e.tensor_mul(
            tA[al][:, :], s3[:, k * TOK:(k + 1) * TOK], rst[:, :]))(),
            reads=[r_s3, r_rst], writes=[r_tA[al]])
        S.op("vector", (lambda al=al: lambda e: e.tensor_sub(
            tA[al][:, :], tA[al][:, :], mu[:, :]))(),
            reads=[r_mu], writes=[r_tA[al]])
        S.op("vector", (lambda al=al: lambda e: e.tensor_add(
            tA[al][:, :], tA[al][:, :], xstg[al][:, :]))(),
            reads=[r_xstg[al]], writes=[r_tA[al]])
        S.op("gpsimd", (lambda al=al, k=k: lambda e: e.dma_start(
            out=d_x1[k * 128:(k + 1) * 128, :], in_=tA[al][:, :]))(),
            reads=[r_tA[al]], writes=[r_x1], dma=True)

    def x1_src(k):
        return ((lambda al=k % 2, k=k: lambda e: e.dma_start(
            out=xstg[al][:, :], in_=d_x1[k * 128:(k + 1) * 128, :]))(), r_x1)

    stats_from(None, None, True, src_f32=x1_src)
    for k in range(16):
        al = k % 2
        S.op("gpsimd", (lambda al=al, k=k: lambda e: e.dma_start(
            out=xstg[al][:, :], in_=d_x1[k * 128:(k + 1) * 128, :]))(),
            reads=[r_x1], writes=[r_xstg[al]], dma=True)
        S.op("vector", (lambda al=al: lambda e: e.tensor_mul(
            tA[al][:, :], xstg[al][:, :], rst[:, :]))(),
            reads=[r_xstg[al], r_rst], writes=[r_tA[al]])
        S.op("vector", (lambda al=al, k=k: lambda e: e.tensor_sub(
            s4[:, k * TOK:(k + 1) * TOK], tA[al][:, :], mu[:, :]))(),
            reads=[r_tA[al], r_mu], writes=[r_s4])

    dump("dbT", s4[:, :], [r_s4])
    if debug:
        S.op("gpsimd", lambda e: e.dma_start(out=ddump["dgbf"][:, :], in_=gbf[:, :]),
             reads=[r_small], writes=[r_yt], dma=True)
        S.op("gpsimd", lambda e: e.dma_start(out=ddump["dgstg"][:, :], in_=gstg[:, :]),
             reads=[r_small], writes=[r_yt], dma=True)
    if debug:
        S.op("gpsimd", lambda e: e.dma_start(out=ddump["dx1"][:, :], in_=d_x1[:, :]),
             reads=[r_x1], writes=[r_yt], dma=True)
    def mm_gateT(e):
        i0 = None
        for k in range(16):
            for hhalf in range(2):
                i0 = e.matmul(psC[0:8, hhalf * 512:(hhalf + 1) * 512],
                              gbf[:, k * 8:(k + 1) * 8],
                              s4[:, k * TOK + hhalf * 512: k * TOK + (hhalf + 1) * 512],
                              start=(k == 0), stop=(k == 15))
        return i0
    S.op("tensor", mm_gateT, reads=[r_s4, r_small], writes=[r_psCb0, r_psCb1])
    S.op("scalar", lambda e: e.copy(lgT[:, :], psC[0:8, :]),
         reads=[r_psCb0, r_psCb1], writes=[r_lgT])
    for tt in range(8):
        rb = r_psCb0 if tt % 2 == 0 else r_psCb1
        psl = psC[0:128, 0:8] if tt % 2 == 0 else psC[0:128, 512:520]
        S.op("tensor", (lambda tt=tt, psl=psl: lambda e: e.transpose(
            psl, lgT[0:8, tt * 128:(tt + 1) * 128], identf[0:8, 0:8]))(),
            reads=[r_lgT, r_small], writes=[rb])
        S.op("scalar", (lambda tt=tt, psl=psl: lambda e: e.copy(
            lgtB[:, tt * 8:(tt + 1) * 8], psl))(),
            reads=[rb], writes=[r_lgtB])

    if inject_lgt:
        S.op("gpsimd", lambda e: e.dma_start(out=lgtB[:, :], in_=d_lgtin[:, :]),
             reads=[r_lgtB], writes=[r_lgtB], dma=True)

    def pair_adds_all(e):
        i0 = None
        for tt in range(8):
            e.memset(s28B[:, tt * 32 + 28: tt * 32 + 32], -240.0)
        for tt in range(8):
            idx = 0
            for i in range(8):
                for j in range(i + 1, 8):
                    i0 = e.tensor_add(s28B[:, tt * 32 + idx: tt * 32 + idx + 1],
                                      lgtB[:, tt * 8 + i: tt * 8 + i + 1],
                                      lgtB[:, tt * 8 + j: tt * 8 + j + 1])
                    idx += 1
        return i0
    S.op("vector", lambda e: pair_adds_all(e), reads=[r_lgtB], writes=[r_s28B])
    for li, w in enumerate((16, 8, 4, 2, 1)):
        def lvl_subadd(e, w=w):
            i0 = None
            for tt in range(8):
                i0 = e.tensor_sub(s28dB[:, tt * 16: tt * 16 + w],
                                  s28B[:, tt * 32: tt * 32 + w],
                                  s28B[:, tt * 32 + w: tt * 32 + 2 * w])
                i0 = e.tensor_add(s28sB[:, tt * 16: tt * 16 + w],
                                  s28B[:, tt * 32: tt * 32 + w],
                                  s28B[:, tt * 32 + w: tt * 32 + 2 * w])
            return i0
        S.op("vector", (lambda w=w: lambda e: lvl_subadd(e, w))(),
             reads=[r_s28B], writes=[r_s28dB, r_s28sB])

        def lvl_abs(e, w=w):
            i0 = None
            for tt in range(8):
                i0 = e.activation(s28aB[:, tt * 16: tt * 16 + w],
                                  s28dB[:, tt * 16: tt * 16 + w], AF.Abs, scale=0.5)
            return i0
        S.op("scalar", (lambda w=w: lambda e: lvl_abs(e, w))(),
             reads=[r_s28dB], writes=[r_s28aB])
        if w > 1:
            def lvl_comb(e, w=w):
                i0 = None
                for tt in range(8):
                    i0 = e.scalar_tensor_tensor(
                        s28B[:, tt * 32: tt * 32 + w],
                        s28sB[:, tt * 16: tt * 16 + w], 0.5,
                        s28aB[:, tt * 16: tt * 16 + w], ALU.mult, ALU.add)
                return i0
            S.op("vector", (lambda w=w: lambda e: lvl_comb(e, w))(),
                 reads=[r_s28sB, r_s28aB], writes=[r_s28B])
        else:
            def lvl_fin(e):
                i0 = None
                for tt in range(8):
                    i0 = e.scalar_tensor_tensor(
                        sctok[:, tt:tt + 1], s28sB[:, tt * 16: tt * 16 + 1], 0.5,
                        s28aB[:, tt * 16: tt * 16 + 1], ALU.mult, ALU.add)
                return i0
            S.op("vector", lambda e: lvl_fin(e),
                 reads=[r_s28sB, r_s28aB], writes=[r_gate])

    if inject_sctok:
        S.op("gpsimd", lambda e: e.dma_start(out=sctok[:, :], in_=d_sctin[:, :]),
             reads=[r_gate], writes=[r_gate], dma=True)
    S.op("tensor", lambda e: e.transpose(psC[0:8, 512:640], sctok[:, :], identf[:, :]),
         reads=[r_gate, r_small], writes=[r_psCb1])
    S.op("scalar", lambda e: e.copy(scstT[:, :], psC[0:8, 512:640]),
         reads=[r_psCb1], writes=[r_gate])
    S.op("gpsimd", lambda e: e.dma_start(out=scT[0:1, :], in_=scstT[:, :]),
         reads=[r_gate], writes=[r_gate, r_scbc], dma=True)

    if debug:
        S.op("gpsimd", lambda e: e.dma_start(out=ddump["dscstT"][:, :], in_=scstT[:, :]),
             reads=[r_gate], writes=[r_yt], dma=True)

    def mm_scbc(e):
        i0 = None
        for hhalf in range(2):
            i0 = e.matmul(psC[:, hhalf * 512:(hhalf + 1) * 512], ones1f[0:1, :],
                          scT[0:1, hhalf * 512:(hhalf + 1) * 512],
                          start=True, stop=True)
        return i0
    S.op("tensor", mm_scbc, reads=[r_gate, r_small, r_scbc], writes=[r_psCb0, r_psCb1])
    S.op("scalar", lambda e: e.copy(scbc[:, :], psC[:, :]),
         reads=[r_psCb0, r_psCb1], writes=[r_scbc])

    if inject_scale:
        S.op("gpsimd", lambda e: e.dma_start(out=scbc[:, :], in_=d_scin[:, :]),
             reads=[r_scbc], writes=[r_scbc], dma=True)
    for m in range(16):
        al = m % 2
        aw = m % 3
        S.op("sync", (lambda aw=aw, m=m: lambda e: e.dma_start(
            out=wstg[aw][:, :], in_=d_wexp[m]))(),
            writes=[r_wstg[aw]], dma=True)
        S.op("vector", (lambda al=al, aw=aw: lambda e: e.tensor_copy(wbf[al][:, :], wstg[aw][:, :]))(),
             reads=[r_wstg[aw]], writes=[r_wbf[al]])

        def mm_moe(e, al=al):
            i0 = None
            for k in range(16):
                for hhalf in range(2):
                    i0 = e.matmul(
                        (psA if al == 0 else psB)[:, hhalf * 512:(hhalf + 1) * 512],
                        wbf[al][:, k * 128:(k + 1) * 128],
                        s4[:, k * TOK + hhalf * 512: k * TOK + (hhalf + 1) * 512],
                        start=(k == 0), stop=(k == 15))
            return i0
        S.op("tensor", (lambda al=al: lambda e: mm_moe(e, al))(),
             reads=[r_wbf[al], r_s4], writes=ps2[al])
        S.op("vector", (lambda al=al, m=m: lambda e: e.tensor_mul(
            s1[:, m * TOK:(m + 1) * TOK], (psA if al == 0 else psB)[:, :], scbc[:, :]))(),
            reads=ps2[al] + [r_scbc], writes=[r_s1])

    for grp in range(4):
        for i in range(16):
            gi = grp * 16 + i
            al = gi % 2
            aw = gi % 3
            S.op("sync", (lambda aw=aw, gi=gi: lambda e: e.dma_start(
                out=wstg[aw][:, :], in_=d_wup[gi]))(),
                writes=[r_wstg[aw]], dma=True)
            S.op("vector", (lambda al=al, aw=aw: lambda e: e.tensor_copy(wbf[al][:, :], wstg[aw][:, :]))(),
                 reads=[r_wstg[aw]], writes=[r_wbf[al]])

            def mm_up(e, al=al):
                i0 = None
                for k in range(16):
                    for hhalf in range(2):
                        i0 = e.matmul(
                            (psA if al == 0 else psB)[:, hhalf * 512:(hhalf + 1) * 512],
                            wbf[al][:, k * 128:(k + 1) * 128],
                            s4[:, k * TOK + hhalf * 512: k * TOK + (hhalf + 1) * 512],
                            start=(k == 0), stop=(k == 15))
                return i0
            S.op("tensor", (lambda al=al: lambda e: mm_up(e, al))(),
                 reads=[r_wbf[al], r_s4], writes=ps2[al])
            S.op("vector", (lambda al=al: lambda e: e.tensor_scalar_max(
                rlb[al][:, :], (psA if al == 0 else psB)[:, :], 0.0))(),
                reads=ps2[al], writes=[r_rlb[al]])
            S.op("scalar", (lambda al=al, i=i: lambda e: e.square(
                s2[:, i * TOK:(i + 1) * TOK], rlb[al][:, :]))(),
                reads=[r_rlb[al]], writes=[r_s2])
        for m in range(16):
            al = m % 2
            aw = m % 3
            S.op("sync", (lambda aw=aw, grp=grp, m=m: lambda e: e.dma_start(
                out=wstg[aw][:, :],
                in_=d_wdown[grp * 16 + m]))(),
                writes=[r_wstg[aw]], dma=True)
            S.op("vector", (lambda al=al, aw=aw: lambda e: e.tensor_copy(wbf[al][:, :], wstg[aw][:, :]))(),
                 reads=[r_wstg[aw]], writes=[r_wbf[al]])

            def mm_down(e, al=al):
                i0 = None
                for i2 in range(16):
                    for hhalf in range(2):
                        i0 = e.matmul(
                            psC[:, hhalf * 512:(hhalf + 1) * 512],
                            wbf[al][:, i2 * 128:(i2 + 1) * 128],
                            s2[:, i2 * TOK + hhalf * 512: i2 * TOK + (hhalf + 1) * 512],
                            start=(i2 == 0), stop=(i2 == 15))
                return i0
            S.op("tensor", (lambda al=al: lambda e: mm_down(e, al))(),
                 reads=[r_wbf[al], r_s2], writes=[r_psCb0, r_psCb1])
            S.op("vector", (lambda m=m: lambda e: e.tensor_add(
                s1[:, m * TOK:(m + 1) * TOK], s1[:, m * TOK:(m + 1) * TOK], psC[:, :]))(),
                reads=[r_psCb0, r_psCb1], writes=[r_s1])

    dump("dy", s1[:, :], [r_s1])
    dump("dscbc", scbc[:, :], [r_scbc])
    stats_from(lambda k: s1[:, k * TOK:(k + 1) * TOK], r_s1, False)
    for k in range(16):
        al = k % 2
        S.op("vector", (lambda al=al, k=k: lambda e: e.tensor_mul(
            tA[al][:, :], s1[:, k * TOK:(k + 1) * TOK], rst[:, :]))(),
            reads=[r_s1, r_rst], writes=[r_tA[al]])
        S.op("vector", (lambda al=al: lambda e: e.tensor_sub(
            tA[al][:, :], tA[al][:, :], mu[:, :]))(),
            reads=[r_mu], writes=[r_tA[al]])
        S.op("vector", (lambda al=al, k=k: lambda e: e.tensor_scalar(
            s3[:, k * TOK:(k + 1) * TOK], tA[al][:, :], gam[:, k:k + 1], bet[:, k:k + 1],
            ALU.mult, ALU.add))(),
            reads=[r_tA[al]], sreads=[r_small], writes=[r_s3])

    dump("df", s3[:, :], [r_s3])
    stats_from(lambda k: s3[:, k * TOK:(k + 1) * TOK], r_s3, False)
    for k in range(16):
        al = k % 2
        S.op("gpsimd", (lambda al=al, k=k: lambda e: e.dma_start(
            out=xstg[al][:, :], in_=d_x1[k * 128:(k + 1) * 128, :]))(),
            reads=[r_x1], writes=[r_xstg[al]], dma=True)
        S.op("vector", (lambda al=al, k=k: lambda e: e.tensor_mul(
            tA[al][:, :], s3[:, k * TOK:(k + 1) * TOK], rst[:, :]))(),
            reads=[r_s3, r_rst], writes=[r_tA[al]])
        S.op("vector", (lambda al=al: lambda e: e.tensor_sub(
            tA[al][:, :], tA[al][:, :], mu[:, :]))(),
            reads=[r_mu], writes=[r_tA[al]])
        S.op("vector", (lambda al=al: lambda e: e.tensor_add(
            tA[al][:, :], tA[al][:, :], xstg[al][:, :]))(),
            reads=[r_xstg[al]], writes=[r_tA[al]])
        S.op("gpsimd", (lambda al=al, k=k: lambda e: e.dma_start(
            out=d_yt[k * 128:(k + 1) * 128, :], in_=tA[al][:, :]))(),
            reads=[r_tA[al]], writes=[r_yt], dma=True)

    S.op("gpsimd", lambda e: e.nop(), reads=[r_yt])

    used = nc.SBUF_PARTITION_SIZE_BYTES - nc.sbuf_bytes_remaining
    assert used <= 224 * 1024, f"SBUF high-water {used} exceeds safe 224KB"

    nc._sched = S
    with nc.Block() as block:
        block.sync(S.replay("sync"))
        block.gpsimd(S.replay("gpsimd"))
        block.tensor(S.replay("tensor"))
        block.scalar(S.replay("scalar"))
        block.vector(S.replay("vector"))
    return nc


_NC_CACHE = None


def _get_nc():
    global _NC_CACHE
    if _NC_CACHE is None:
        _NC_CACHE = build_nc()
    return _NC_CACHE


def _tile_w(w):
    K, M = w.shape
    wt = w.reshape(16, 128, M).transpose(1, 0, 2)
    out = wt.reshape(128, 16, M // 128, 128).transpose(2, 0, 1, 3)
    return np.ascontiguousarray(out.reshape(M // 128, 128, 2048), dtype=np.float32)


def kernel(L0_attn_x, L0_attn_Wqkv, L0_attn_bqkv, L0_attn_Wproj, L0_attn_bproj,
           L0_ffn_gate_w, L0_ffn_up_w, L0_ffn_down_w, L0_ffn_expert_weight,
           L0_ffn_ln_gamma, L0_ffn_ln_beta):
    x = np.asarray(L0_attn_x, dtype=np.float32)[0]
    Wqkv = np.asarray(L0_attn_Wqkv, dtype=np.float32)
    bqkv = np.asarray(L0_attn_bqkv, dtype=np.float32)
    Wproj = np.asarray(L0_attn_Wproj, dtype=np.float32)
    bproj = np.asarray(L0_attn_bproj, dtype=np.float32)
    gate_w = np.asarray(L0_ffn_gate_w, dtype=np.float32)
    up_w = np.asarray(L0_ffn_up_w, dtype=np.float32)
    down_w = np.asarray(L0_ffn_down_w, dtype=np.float32)
    exp_w = np.asarray(L0_ffn_expert_weight, dtype=np.float32)
    gam = np.asarray(L0_ffn_ln_gamma, dtype=np.float32)
    bet = np.asarray(L0_ffn_ln_beta, dtype=np.float32)

    wqkv_t = _tile_w(Wqkv)
    wproj_t = _tile_w(Wproj)
    wup_t = _tile_w(up_w)
    wexp_t = _tile_w(exp_w)
    dn = down_w.reshape(4, 16, 128, 16, 128)
    wdown_t = np.ascontiguousarray(
        dn.transpose(0, 3, 2, 1, 4).reshape(64, 128, 2048), dtype=np.float32)
    bqkv_t = np.ascontiguousarray(bqkv.reshape(48, 128).T, dtype=np.float32)
    bproj_t = np.ascontiguousarray(bproj.reshape(16, 128).T, dtype=np.float32)
    gate_t = np.ascontiguousarray(
        gate_w.reshape(16, 128, 8).transpose(1, 0, 2).reshape(128, 128),
        dtype=np.float32)
    gam_t = np.ascontiguousarray(gam.reshape(16, 128).T, dtype=np.float32)
    bet_t = np.ascontiguousarray(bet.reshape(16, 128).T, dtype=np.float32)
    mask_t = np.kron(np.eye(8, dtype=np.float32),
                     np.ones((16, 16), np.float32)).astype(BF)
    ident_t = np.eye(128, dtype=np.float32).astype(BF)

    L = np.arange(1024)
    in_maps = []
    s2_rows = []
    for c in range(NCORE):
        s2s = (L & 15) * 512 + 64 * c + (L >> 4)
        s2_rows.append(s2s)
        in_maps.append({
            "xt": np.ascontiguousarray(x[c * 1024:(c + 1) * 1024, :].T),
            "xpt": np.ascontiguousarray(x[s2s, :].T),
            "wqkv": wqkv_t, "bqkv": bqkv_t,
            "wproj": wproj_t, "bproj": bproj_t,
            "gate": gate_t, "wup": wup_t, "wdown": wdown_t, "wexp": wexp_t,
            "gam": gam_t, "bet": bet_t,
            "mask": mask_t, "ident": ident_t,
        })

    nc = _get_nc()
    res = run_bass_kernel_spmd(nc, in_maps, core_ids=list(range(NCORE)))
    out = np.empty((1, 8192, 2048), dtype=np.float32)
    for c in range(NCORE):
        out[0, s2_rows[c], :] = res.results[c]["yt"].T
    return out
